# version 10 (final)
# baseline (speedup 1.0000x reference)
import numpy as np
from contextlib import ExitStack

import concourse.bass as bass
import concourse.mybir as mybir
from concourse.bass_utils import run_bass_kernel_spmd

F32 = mybir.dt.float32
BF16 = mybir.dt.bfloat16
I32 = mybir.dt.int32
AF = mybir.ActivationFunctionType
ALU = mybir.AluOpType
AXX = mybir.AxisListType.X

EPS = 1e-5
T = 4096
OWN = 2048
D = 1024
NCH = 8
MCH512 = [8, 4, 2, 1, 8, 4, 2, 1]
LN10K = float(np.log(10000.0))
PI = float(np.pi)
TWO_PI = 2.0 * np.pi
C1 = float(np.float32(TWO_PI))
_r1 = np.float64(TWO_PI) - np.float64(np.float32(TWO_PI))
C2 = float(np.float32(_r1))
C3 = float(np.float32(np.float64(_r1) - np.float64(np.float32(_r1))))
PIH1 = float(np.float32(np.pi / 2))
PIH2 = float(np.float32(np.float64(np.pi / 2) - np.float64(np.float32(np.pi / 2))))
_A = -np.log(10000.0) / 128.0
A1 = float(np.float32(_A))
A2 = float(np.float32(np.float64(_A) - np.float64(np.float32(_A))))

PHASES = 99
SUPS = 4
DEBUG = ()


def _perm_for(h):
    own = np.concatenate([np.arange(e * 1024 + h * 512, e * 1024 + h * 512 + 512) for e in range(4)])
    par = np.concatenate([np.arange(e * 1024 + (1 - h) * 512, e * 1024 + (1 - h) * 512 + 512) for e in range(4)])
    return np.concatenate([own, par])


def build_nc():
    nc = bass.Bass("TRN2", target_bir_lowering=False, detect_race_conditions=False)
    st = ExitStack()

    def dram_in(name, shape):
        return nc.dram_tensor(name, shape, F32, kind="ExternalInput").ap()

    xT = dram_in("xT", [D, T])
    posrow = dram_in("posrow", [1, OWN])
    poscol = dram_in("poscol", [128, 32])
    WqkT = dram_in("WqkT", [D, 256])
    WvT = dram_in("WvT", [D, 2048])
    WgT = dram_in("WgT", [D, 2048])
    WoT = dram_in("WoT", [2048, D])
    qwT = dram_in("qwT", [D, D])
    kwT = dram_in("kwT", [D, D])
    vwT = dram_in("vwT", [D, D])
    owT = dram_in("owT", [D, D])
    pos_emb = dram_in("pos_emb", [1, 65])
    gammaC = dram_in("gammaC", [128, 8])
    betaC = dram_in("betaC", [128, 8])
    g3_row = dram_in("g3_row", [1, 256])
    b3_row = dram_in("b3_row", [1, 256])
    bqk_row = dram_in("bqk_row", [1, 256])
    bqk_col = dram_in("bqk_col", [128, 2])
    bv_row = dram_in("bv_row", [1, 2048])
    bg_row = dram_in("bg_row", [1, 2048])
    vb_row = dram_in("vb_row", [1, D])
    ln1w_c = dram_in("ln1w_c", [128, 8]); ln1b_c = dram_in("ln1b_c", [128, 8])
    lnfw_c = dram_in("lnfw_c", [128, 8]); lnfb_c = dram_in("lnfb_c", [128, 8])
    ln2w_c = dram_in("ln2w_c", [128, 8]); ln2b_c = dram_in("ln2b_c", [128, 8])
    qb_c = dram_in("qb_c", [128, 8]); kb_c = dram_in("kb_c", [128, 8])
    bo_c = dram_in("bo_c", [128, 8]); ob_c = dram_in("ob_c", [128, 8])

    outT = nc.dram_tensor("outT", [D, OWN], F32, kind="ExternalOutput").ap()

    vhbm = nc.dram_tensor("vhbm", [T, 2048], BF16, kind="Internal").ap()
    khbm = nc.dram_tensor("khbm", [8, 128, T], BF16, kind="Internal").ap()
    qhbm = nc.dram_tensor("qhbm", [8, 128, OWN], BF16, kind="Internal").ap()
    attnhbm = nc.dram_tensor("attnhbm", [8, 128, OWN], BF16, kind="Internal").ap()
    gauhbm = nc.dram_tensor("gauhbm", [8, 128, OWN], BF16, kind="Internal").ap()

    dbg = {}

    def dbg_sb(name, src_ap, rows, cols):
        if name not in DEBUG:
            return
        dt = nc.dram_tensor("dbg_" + name, [rows, cols], F32, kind="ExternalOutput").ap()
        dbg[name] = dt
        with nc.Block("dbg_" + name) as blk:
            pdd = ExitStack()
            bounce = pdd.enter_context(nc.sbuf_tensor("dbB" + name, [rows, cols], F32, side="right")).ap()
            sdd = st.enter_context(nc.semaphore("dbS" + name))

            @blk.vector
            def _(ve):
                ve.tensor_copy(bounce[:, :], src_ap).then_inc(sdd, 1)

            @blk.sync
            def _(sy):
                sy.wait_ge(sdd, 1)
                sy.dma_start(out=dt[:, :], in_=bounce[:, :]).then_inc(sdd, 16)
                sy.wait_ge(sdd, 17)
            pdd.close()

    def dbg_hbm(name, src_dram, nchunks, rows, cols):
        if name not in DEBUG:
            return
        dt = nc.dram_tensor("dbg_" + name, [nchunks, rows, cols], F32, kind="ExternalOutput").ap()
        dbg[name] = dt
        with nc.Block("dbg_" + name) as blk:
            pdd = ExitStack()
            bb = pdd.enter_context(nc.sbuf_tensor("dhB" + name, [rows, cols], BF16, side="right")).ap()
            bf = pdd.enter_context(nc.sbuf_tensor("dhF" + name, [rows, cols], F32, side="right")).ap()
            sdd = st.enter_context(nc.semaphore("dhS" + name))
            sdo = st.enter_context(nc.semaphore("dhO" + name))

            @blk.sync
            def _(sy):
                for i in range(nchunks):
                    if i > 0:
                        sy.wait_ge(sdd, i * 17)
                    sy.dma_start(out=bb[:, :], in_=src_dram[i] if nchunks > 1 else src_dram[:, :]).then_inc(sdd, 16)

            @blk.vector
            def _(ve):
                for i in range(nchunks):
                    ve.wait_ge(sdd, i * 17 + 16)
                    if i > 0:
                        ve.wait_ge(sdo, 16 * i)
                    ve.tensor_copy(bf[:, :], bb[:, :]).then_inc(sdd, 1)

            @blk.gpsimd
            def _(gp):
                for i in range(nchunks):
                    gp.wait_ge(sdd, i * 17 + 17)
                    gp.dma_start(out=dt[i] if nchunks > 1 else dt[:, :], in_=bf[:, :]).then_inc(sdo, 16)
                gp.wait_ge(sdo, nchunks * 16)
            pdd.close()

    def sb(name, shape, dt=BF16):
        return nc.alloc_sbuf_tensor(name, shape, dt).ap()

    MU = sb("MU", [128, T])
    RSTD = sb("RSTD", [128, T])
    
    ones128 = sb("ones128", [128, 128])
    ones1 = sb("ones1", [1, 128])
    ones1f = sb("ones1f", [1, 128], F32)
    onescol = sb("onescol", [128, 1])
    ident = sb("ident", [128, 128])
    identF = sb("identF", [128, 128], F32)
    qbias = sb("qbias", [128, 64], F32)
    linkv = sb("linkv", [128, 2 * 2048])
    negB = sb("negB", [128, 32], F32)
    KMax2 = sb("KMax2", [128, 8], F32)
    QMax2 = sb("QMax2", [128, 32], F32)
    ln1w = sb("ln1w", [128, 8], F32); ln1b = sb("ln1b", [128, 8], F32)
    lnfw = sb("lnfw", [128, 8], F32); lnfb = sb("lnfb", [128, 8], F32)
    ln2w = sb("ln2w", [128, 8], F32); ln2b = sb("ln2b", [128, 8], F32)
    qbC = sb("qbC", [128, 8], F32); kbC = sb("kbC", [128, 8], F32)
    boC = sb("boC", [128, 8], F32); obC = sb("obC", [128, 8], F32)
    bqkC = sb("bqkC", [128, 2], F32)
    gamC = sb("gamC", [128, 8], F32); betC = sb("betC", [128, 8], F32)
    bqkR = sb("bqkR", [1, 256]); bvR = sb("bvR", [1, 2048])
    bqkP = sb("bqkP", [1, 256])
    ln1bB = sb("ln1bB", [128, 8]); onesRow = sb("onesRow", [1, 512])
    bgR = sb("bgR", [1, 2048]); vbR = sb("vbR", [1, D])
    g3R = sb("g3R", [1, 256], F32); b3R = sb("b3R", [1, 256], F32)
    peR = sb("peR", [1, 65], F32)
    posC = sb("posC", [128, 32], F32)
    invfR = sb("invfR", [1, 128], F32)

    trF_g = nc.sbuf_tensor("trF", [128, 2 * OWN], BF16); trF = trF_g.__enter__().ap()
    sinF = trF[:, 0:OWN]; cosF = trF[:, OWN:2 * OWN]
    z_g = nc.sbuf_tensor("z", [128, NCH * T], BF16); z = z_g.__enter__().ap()
    lk_g = nc.sbuf_tensor("lk", [128, 32 * 256], BF16); lk = lk_g.__enter__().ap()
    tmT_g = nc.sbuf_tensor("tmT", [128, 2 * 32 * 128], BF16); tmT = tmT_g.__enter__().ap()
    sinTt = tmT[:, 0:32 * 128]; cosTt = tmT[:, 32 * 128:]

    def semN(name):
        return st.enter_context(nc.semaphore(name))

    sX = semN("sX")
    sW = semN("sW")
    wctr = [0]

    def wload(gp, dst, src_dram, rearr=True):
        src = src_dram.rearrange("(c p) k -> p c k", p=128) if rearr else src_dram
        gp.dma_start(out=dst, in_=src).then_inc(sW, 16)
        wctr[0] += 16
        return wctr[0]

    p0 = ExitStack()
    posR = p0.enter_context(nc.sbuf_tensor("posR", [1, OWN], F32, side="right")).ap()
    RPf = p0.enter_context(nc.sbuf_tensor("RPf", [128, T], BF16, side="right")).ap()
    pbc = p0.enter_context(nc.psum_tensor("pbc", [128, 2 * 512], F32)).ap()
    ppe = p0.enter_context(nc.psum_tensor("ppe", [64, 65], F32)).ap()
    p0t = ExitStack()
    phF = p0t.enter_context(nc.sbuf_tensor("phF", [128, T], F32, side="right")).ap()
    tmpF = p0t.enter_context(nc.sbuf_tensor("tmpF", [128, OWN], F32, side="right")).ap()
    kiF = p0t.enter_context(nc.sbuf_tensor("kiF", [128, OWN], I32, side="right")).ap()
    dist = p0t.enter_context(nc.sbuf_tensor("dist", [64, 64], F32, side="right")).ap()
    msk = p0t.enter_context(nc.sbuf_tensor("msk", [64, 64], F32, side="right")).ap()
    peB = p0t.enter_context(nc.sbuf_tensor("peB", [64, 65], F32, side="right")).ap()
    invfB = p0t.enter_context(nc.sbuf_tensor("invfB", [128, 128], F32, side="right")).ap()
    invfW = p0t.enter_context(nc.sbuf_tensor("invfW", [128, 128], F32, side="right")).ap()

    with nc.Block("p0a") as blk:
        @blk.gpsimd
        def _(gp):
            for c in range(NCH):
                gp.dma_start(out=z[:, c * T:(c + 1) * T],
                             in_=xT[c * 128:(c + 1) * 128, :],
                             max_dma_last_dim=2048).then_inc(sX, 16)
            loads = [(posR, posrow), (posC, poscol), (peR, pos_emb),
                     (ln1w, ln1w_c), (ln1b, ln1b_c), (lnfw, lnfw_c), (lnfb, lnfb_c),
                     (ln2w, ln2w_c), (ln2b, ln2b_c), (qbC, qb_c), (kbC, kb_c),
                     (boC, bo_c), (obC, ob_c), (bqkC, bqk_col),
                     (gamC, gammaC), (betC, betaC),
                     (bqkR, bqk_row), (bvR, bv_row), (bgR, bg_row), (vbR, vb_row),
                     (g3R, g3_row), (b3R, b3_row)]
            for dst, src in loads:
                wload(gp, dst[:, :], src, rearr=False)
            gp.memset(ones128[:, :], 1.0)
            gp.memset(ones1[:, :], 1.0)
            gp.memset(ones1f[:, :], 1.0)
            gp.memset(onesRow[:, :], 1.0)
            gp.memset(onescol[:, :], 1.0)
            sid = semN("p0a_id")
            gp.memset(ident[:, :], 1.0)
            gp.memset(identF[:, :], 1.0).then_inc(sid, 1)
            gp.wait_ge(sid, 1)
            gp.affine_select(ident[:, :], ident[:, :], [[1, 128]], ALU.is_equal, 0.0,
                             channel_multiplier=-1)
            gp.affine_select(identF[:, :], identF[:, :], [[1, 128]], ALU.is_equal, 0.0,
                             channel_multiplier=-1)
            gp.iota(invfW[:, :], [[0, 128]], channel_multiplier=1,
                    allow_small_or_imprecise_dtypes=True)
            gp.iota(invfR[:, :], [[1, 128]], channel_multiplier=0,
                    allow_small_or_imprecise_dtypes=True)
            gp.wait_ge(sW, wctr[0])


    def wrap_pm_pi(ve, dstap, tmp1, tmpi):
        ve.tensor_single_scalar(tmp1, dstap, float(1.0 / TWO_PI), ALU.mult)
        ve.tensor_single_scalar(tmp1, tmp1, 0.5, ALU.add)
        ve.tensor_copy(tmpi, tmp1)
        ve.tensor_copy(tmp1, tmpi)
        ve.scalar_tensor_tensor(dstap, tmp1, -C1, dstap, ALU.mult, ALU.add)
        ve.scalar_tensor_tensor(dstap, tmp1, -C2, dstap, ALU.mult, ALU.add)
        ve.scalar_tensor_tensor(dstap, tmp1, -C3, dstap, ALU.mult, ALU.add)
        ve.tensor_single_scalar(tmp1, dstap, PI, ALU.is_gt)
        ve.scalar_tensor_tensor(dstap, tmp1, -C1, dstap, ALU.mult, ALU.add)
        ve.scalar_tensor_tensor(dstap, tmp1, -C2, dstap, ALU.mult, ALU.add)
        ve.scalar_tensor_tensor(dstap, tmp1, -C3, dstap, ALU.mult, ALU.add)
        ve.tensor_single_scalar(tmp1, dstap, -PI, ALU.is_lt)
        ve.scalar_tensor_tensor(dstap, tmp1, C1, dstap, ALU.mult, ALU.add)
        ve.scalar_tensor_tensor(dstap, tmp1, C2, dstap, ALU.mult, ALU.add)
        ve.scalar_tensor_tensor(dstap, tmp1, C3, dstap, ALU.mult, ALU.add)

    def shift_cos(ve, dstap, srcap, tmp1):
        ve.tensor_single_scalar(dstap, srcap, PIH1, ALU.add)
        ve.tensor_single_scalar(dstap, dstap, PIH2, ALU.add)
        ve.tensor_single_scalar(tmp1, dstap, PI, ALU.is_gt)
        ve.scalar_tensor_tensor(dstap, tmp1, -C1, dstap, ALU.mult, ALU.add)
        ve.scalar_tensor_tensor(dstap, tmp1, -C2, dstap, ALU.mult, ALU.add)
        ve.scalar_tensor_tensor(dstap, tmp1, -C3, dstap, ALU.mult, ALU.add)

    with nc.Block("p0b0") as blk:
        s0b = semN("p0b0_s")

        @blk.vector
        def _(ve):
            ve.tensor_single_scalar(invfB[:, :], invfW[:, :], A2, ALU.mult)
            ve.tensor_single_scalar(invfW[:, :], invfW[:, :], A1, ALU.mult)
            ve.tensor_add(invfW[:, :], invfW[:, :], invfB[:, :])
            ve.tensor_single_scalar(tmpF[0:1, 0:128], invfR[:, :], A2, ALU.mult)
            ve.tensor_single_scalar(invfR[:, :], invfR[:, :], A1, ALU.mult)
            ve.tensor_add(invfR[:, :], invfR[:, :], tmpF[0:1, 0:128])
            ve.engine_nop()
            ve.engine_nop().then_inc(s0b, 1)

    with nc.Block("p0b1") as blk:
        @blk.scalar
        def _(sc):
            sc.wait_ge(s0b, 1)
            sc.activation(invfW[:, :], invfW[:, :], AF.Exp)
            sc.activation(invfR[:, :], invfR[:, :], AF.Exp)

        @blk.gpsimd
        def _(gp):
            gp.iota(dist[:, :], [[1, 64]], channel_multiplier=-1,
                    allow_small_or_imprecise_dtypes=True)

    with nc.Block("p0b2") as blk:
        s1 = semN("p0b_t"); s2 = semN("p0b_v")

        @blk.tensor
        def _(te):
            for i in range(4):
                if i >= 2:
                    te.wait_ge(s2, i - 1)
                te.matmul(pbc[:, (i % 2) * 512:(i % 2) * 512 + 512], ones1f[:, :],
                          posR[:, i * 512:(i + 1) * 512], start=True, stop=True).then_inc(s1, 1)
            te.matmul(ppe[:, :], ones1f[:, 0:64], peR[:, :], start=True, stop=True).then_inc(s1, 1)

        @blk.vector
        def _(ve):
            for i in range(4):
                ve.wait_ge(s1, i + 1)
                ve.tensor_scalar_mul(phF[:, i * 512:(i + 1) * 512],
                                     pbc[:, (i % 2) * 512:(i % 2) * 512 + 512],
                                     invfW[:, 0:1]).then_inc(s2, 1)
            wrap_pm_pi(ve, phF[:, 0:OWN], tmpF[:, :], kiF[:, :])

    with nc.Block("p0b3") as blk:
        @blk.scalar
        def _(sc):
            sc.activation(sinF[:, :], phF[:, 0:OWN], AF.Sin)
            sc.activation(dist[:, :], dist[:, :], AF.Abs)

    with nc.Block("p0b3b") as blk:
        @blk.vector
        def _(ve):
            shift_cos(ve, phF[:, 0:OWN], phF[:, 0:OWN], tmpF[:, :])

    with nc.Block("p0b3c") as blk:
        @blk.scalar
        def _(sc):
            sc.activation(cosF[:, :], phF[:, 0:OWN], AF.Sin)

    with nc.Block("p0b4") as blk:
        s1 = semN("p0b4_t")

        @blk.tensor
        def _(te):
            te.matmul(pbc[:, 0:128], ones1f[:, :], invfR[:, :], start=True, stop=True).then_inc(s1, 1)

        @blk.vector
        def _(ve):
            ve.wait_ge(s1, 1)
            ve.tensor_copy(invfB[:, :], pbc[:, 0:128])
            for b_ in range(32):
                ve.tensor_scalar_mul(phF[:, b_ * 128:(b_ + 1) * 128], invfB[:, :],
                                     posC[:, b_:b_ + 1])
            wrap_pm_pi(ve, phF[:, 0:OWN], tmpF[:, :], kiF[:, :])
            wrap_pm_pi(ve, phF[:, OWN:T], tmpF[:, :], kiF[:, :])

    with nc.Block("p0b5") as blk:
        @blk.scalar
        def _(sc):
            sc.activation(sinTt[:, :], phF[:, :], AF.Sin)

        @blk.vector
        def _(ve):
            ve.tensor_single_scalar(dist[:, :], dist[:, :], 32.0, ALU.min)
            ve.tensor_copy(peB[:, :], ppe[:, :])
            ve.memset(qbias[0:64, :], 0.0)
            for r in range(33):
                ve.tensor_single_scalar(msk[:, :], dist[:, :], float(r), ALU.is_equal)
                ve.scalar_tensor_tensor(qbias[0:64, :], msk[:, :], peB[:, r:r + 1],
                                        qbias[0:64, :], ALU.mult, ALU.add)

    with nc.Block("p0b6") as blk:
        @blk.vector
        def _(ve):
            shift_cos(ve, phF[:, 0:OWN], phF[:, 0:OWN], tmpF[:, :])
            shift_cos(ve, phF[:, OWN:T], phF[:, OWN:T], tmpF[:, :])

        @blk.gpsimd
        def _(gp):
            sq = semN("p0b6_g")
            gp.dma_start(out=qbias[64:128, :], in_=qbias[0:64, :]).then_inc(sq, 16)
            gp.wait_ge(sq, 16)

    with nc.Block("p0b7") as blk:
        @blk.scalar
        def _(sc):
            sc.activation(cosTt[:, :], phF[:, :], AF.Sin)

    p0t.close()
    dbg_sb("sinF", sinF[:, :], 128, OWN)
    dbg_sb("cosF", cosF[:, :], 128, OWN)
    dbg_sb("sinT", sinTt[:, :], 128, T)
    dbg_sb("qbias", qbias[:, :], 128, 64)

    if PHASES >= 1:
        x2 = p0.enter_context(nc.sbuf_tensor("x2", [128, 2 * 512], BF16, side="right")).ap()
        st6 = p0.enter_context(nc.sbuf_tensor("st6", [128, 8 * 512], F32, side="right")).ap()
        psx = p0.enter_context(nc.psum_tensor("psx", [128, 2 * 512], F32)).ap()
        psx2 = p0.enter_context(nc.psum_tensor("psx2", [128, 2 * 512], F32)).ap()
        MM_ON = ST_ON = RT_ON = Z2_ON = True
        with nc.Block("p0c") as blk:
            sa = semN("p0c_a"); stn = semN("p0c_t"); sv = semN("p0c_v"); st2 = semN("p0c_t2")

            sc_order = []
            for i in range(8):
                for c in range(NCH):
                    sc_order.append(("sq", i * 8 + c))
                if i >= 1:
                    sc_order.append(("rt", i - 1))
            sc_order.append(("rt", 7))
            sq_pos = {k: sc_order.index(("sq", k)) + 1 for k in range(64)}
            rt_pos = {i: sc_order.index(("rt", i)) + 1 for i in range(8)}

            @blk.scalar
            def _(sc):
                sc.wait_ge(sX, 16 * NCH)
                for kind, k in sc_order:
                    if kind == "sq":
                        i, c = k // 8, k % 8
                        if k >= 2 and MM_ON:
                            sc.wait_ge(st2, k - 1)
                        sc.activation(x2[:, (k % 2) * 512:(k % 2) * 512 + 512],
                                      z[:, c * T + i * 512:c * T + (i + 1) * 512],
                                      AF.Square).then_inc(sa, 1)
                    elif RT_ON:
                        sc.wait_ge(sv, 2 * k + 1)
                        sc.activation(st6[:, 2 * 512:3 * 512], st6[:, 1 * 512:2 * 512], AF.Sqrt)
                        sc.activation(st6[:, 5 * 512:6 * 512], st6[:, 4 * 512:5 * 512], AF.Sqrt).then_inc(sa, 1)
                    else:
                        sc.nop().then_inc(sa, 1)

            @blk.tensor
            def _(te):
                if not MM_ON:
                    return
                k = 0
                for i in range(8):
                    pp = i % 2
                    if i >= 2 and ST_ON:
                        te.wait_ge(sv, 2 * (i - 2) + 1)
                    for c in range(NCH):
                        te.wait_ge(sa, sq_pos[k])
                        te.matmul(psx[:, pp * 512:(pp + 1) * 512], ones128[:, :],
                                  z[:, c * T + i * 512:c * T + (i + 1) * 512],
                                  start=(c == 0), stop=(c == NCH - 1))
                        te.matmul(psx2[:, pp * 512:(pp + 1) * 512], ones128[:, :],
                                  x2[:, (k % 2) * 512:(k % 2) * 512 + 512],
                                  start=(c == 0), stop=(c == NCH - 1)).then_inc(st2, 1)
                        k += 1
                    te.wait_ge(st2, 8 * (i + 1))
                    te.nop().then_inc(stn, 1)

            @blk.vector
            def _(ve):
                if not ST_ON:
                    return
                sl = [st6[:, j * 512:(j + 1) * 512] for j in range(8)]
                for i in range(8):
                    ve.wait_ge(stn, i + 1)
                    pp = i % 2
                    px = psx[:, pp * 512:(pp + 1) * 512]
                    px2 = psx2[:, pp * 512:(pp + 1) * 512]
                    ve.tensor_single_scalar(sl[0], px, 1.0 / D, ALU.mult)
                    ve.tensor_single_scalar(sl[1], px2, 1.0 / D, ALU.mult)
                    ve.tensor_tensor(sl[3], sl[0], sl[0], ALU.mult)
                    ve.tensor_sub(sl[1], sl[1], sl[3])
                    ve.tensor_single_scalar(sl[4], sl[1], EPS, ALU.mult)
                    ve.tensor_add(sl[4], sl[4], sl[1])
                    ve.tensor_single_scalar(sl[4], sl[4], EPS * EPS, ALU.add)
                    ve.tensor_single_scalar(sl[1], sl[1], EPS, ALU.add)
                    ve.tensor_copy(MU[:, i * 512:(i + 1) * 512], sl[0])
                    ve.engine_nop()
                    ve.engine_nop().then_inc(sv, 1)
                    if not RT_ON:
                        ve.engine_nop().then_inc(sv, 1)
                        continue
                    ve.wait_ge(sa, rt_pos[i])
                    ve.reciprocal(sl[3], sl[2])
                    ve.tensor_tensor(sl[6], sl[3], sl[3], ALU.mult)
                    ve.tensor_tensor(sl[6], sl[6], sl[1], ALU.mult)
                    ve.tensor_scalar(sl[6], sl[6], -0.5, 1.5, ALU.mult, ALU.add)
                    ve.tensor_tensor(sl[3], sl[3], sl[6], ALU.mult)
                    ve.tensor_copy(RSTD[:, i * 512:(i + 1) * 512], sl[3])
                    ve.reciprocal(sl[3], sl[5])
                    ve.tensor_tensor(sl[6], sl[3], sl[3], ALU.mult)
                    ve.tensor_tensor(sl[6], sl[6], sl[4], ALU.mult)
                    ve.tensor_scalar(sl[6], sl[6], -0.5, 1.5, ALU.mult, ALU.add)
                    ve.tensor_tensor(sl[3], sl[3], sl[6], ALU.mult)
                    ve.tensor_copy(RPf[:, i * 512:(i + 1) * 512], sl[3])
                    ve.engine_nop().then_inc(sv, 1)

        with nc.Block("p0c2") as blk:
            @blk.vector
            def _(ve):
                if not Z2_ON:
                    return
                for c in range(NCH):
                    ve.tensor_sub(z[:, c * T:(c + 1) * T], z[:, c * T:(c + 1) * T], MU[:, :])
                    ve.tensor_mul(z[:, c * T:(c + 1) * T], z[:, c * T:(c + 1) * T], RPf[:, :])

        p0.close()
        dbg_sb("MU", MU[0:1, :], 1, T)
        dbg_sb("RSTD", RSTD[0:1, :], 1, T)
        for c in range(NCH):
            dbg_sb("z%d" % c, z[:, c * T:(c + 1) * T], 128, T)

    if PHASES >= 2:
        pa1 = ExitStack()
        Wqkb = pa1.enter_context(nc.sbuf_tensor("Wqkb", [128, 8 * 256], BF16, side="right")).ap()
        G3b = pa1.enter_context(nc.sbuf_tensor("G3b", [128, 256], BF16, side="right")).ap()
        B3b = pa1.enter_context(nc.sbuf_tensor("B3b", [128, 256], BF16, side="right")).ap()
        qksb = pa1.enter_context(nc.sbuf_tensor("qksb", [128, 2 * 256], BF16, side="right")).ap()
        lkp = pa1.enter_context(nc.sbuf_tensor("lkp", [128, 256], F32, side="right")).ap()
        pqk = pa1.enter_context(nc.psum_tensor("pqk", [128, 2 * 512], F32)).ap()

        with nc.Block("pa1_load") as blk:
            @blk.gpsimd
            def _(gp):
                w = wload(gp, Wqkb[:, :], WqkT)
                gp.wait_ge(sW, w)

        with nc.Block("pa1_pre") as blk:
            s1 = semN("pa1p_t"); s2 = semN("pa1p_v")

            @blk.vector
            def _(ve):
                ve.tensor_copy(ln1bB[:, :], ln1b[:, :]).then_inc(s2, 1)

            @blk.tensor
            def _(te):
                te.matmul(pqk[:, 0:256], ones1f[:, :], g3R[:, :], start=True, stop=True)
                te.matmul(pqk[:, 512:768], ones1f[:, :], b3R[:, :], start=True, stop=True).then_inc(s1, 1)
                te.wait_ge(s2, 1)
                for c in range(NCH):
                    last = te.matmul(pqk[0:1, 768:1024], ln1bB[:, c:c + 1], Wqkb[:, c * 256:(c + 1) * 256],
                                     start=(c == 0), stop=(c == NCH - 1))
                last.then_inc(s1, 1)

            @blk.vector
            def _(ve):
                ve.wait_ge(s1, 2)
                ve.tensor_copy(G3b[:, :], pqk[:, 0:256])
                ve.tensor_copy(B3b[:, :], pqk[:, 512:768])
                ve.tensor_add(bqkP[:, :], pqk[0:1, 768:1024], bqkR[:, :])
                for c in range(NCH):
                    ve.tensor_scalar_mul(Wqkb[:, c * 256:(c + 1) * 256],
                                         Wqkb[:, c * 256:(c + 1) * 256], ln1w[:, c:c + 1])
                ve.engine_nop().then_inc(s2, 1)

        with nc.Block("pa1") as blk:
            stn = semN("pa1_t"); sa = semN("pa1_a"); sv = semN("pa1_v")

            @blk.tensor
            def _(te):
                for b_ in range(32):
                    pp = b_ % 2
                    if b_ >= 2:
                        te.wait_ge(sa, b_ - 1)
                    te.matmul(pqk[:, pp * 512:pp * 512 + 256], ones1[:, :], bqkP[:, :],
                              start=True, stop=False)
                    for c in range(NCH):
                        last = te.matmul(pqk[:, pp * 512:pp * 512 + 256],
                                         z[:, c * T + b_ * 128:c * T + b_ * 128 + 128],
                                         Wqkb[:, c * 256:(c + 1) * 256],
                                         start=False, stop=(c == NCH - 1))
                    last.then_inc(stn, 1)

            @blk.scalar
            def _(sc):
                for b_ in range(32):
                    pp = b_ % 2
                    sc.wait_ge(stn, b_ + 1)
                    if b_ >= 2:
                        sc.wait_ge(sv, b_ - 1)
                    sc.activation(qksb[:, pp * 256:(pp + 1) * 256],
                                  pqk[:, pp * 512:pp * 512 + 256], AF.Silu).then_inc(sa, 1)

            @blk.vector
            def _(ve):
                for b_ in range(32):
                    pp = b_ % 2
                    ve.wait_ge(sa, b_ + 1)
                    q_ = qksb[:, pp * 256:(pp + 1) * 256]
                    ve.tensor_mul(lkp[:, :], q_, G3b[:, :])
                    ve.tensor_add(lkp[:, :], lkp[:, :], B3b[:, :])
                    x1 = lkp[:, 0:128]; x2_ = lkp[:, 128:256]
                    co = cosTt[:, b_ * 128:(b_ + 1) * 128]
                    si = sinTt[:, b_ * 128:(b_ + 1) * 128]
                    o1 = lk[:, b_ * 256:b_ * 256 + 128]
                    o2 = lk[:, b_ * 256 + 128:b_ * 256 + 256]
                    ve.tensor_mul(o1, x1, co)
                    ve.tensor_mul(o2, x2_, si)
                    ve.tensor_sub(o1, o1, o2)
                    ve.tensor_mul(o2, x2_, co)
                    ve.tensor_mul(x1, x1, si)
                    ve.tensor_add(o2, o2, x1)
                    ve.engine_nop().then_inc(sv, 1)
        pa1.close()
        tmT_g.__exit__(None, None, None)
        dbg_sb("lk", lk[:, :], 128, 32 * 256)

    if PHASES >= 3:
        pa2 = ExitStack()
        Wvb = pa2.enter_context(nc.sbuf_tensor("Wvb", [128, 8 * 2048], BF16, side="right")).ap()
        vasm = pa2.enter_context(nc.sbuf_tensor("vasm", [128, 2 * 2048], BF16, side="right")).ap()
        bvP = pa2.enter_context(nc.sbuf_tensor("bvP", [1, 2048], BF16, side="right")).ap()
        pv = pa2.enter_context(nc.psum_tensor("pv", [128, 2 * 1024], F32)).ap()
        plk = pa2.enter_context(nc.psum_tensor("plk", [128, 4 * 512], F32)).ap()

        with nc.Block("pa2_load") as blk:
            @blk.gpsimd
            def _(gp):
                w = wload(gp, Wvb[:, :], WvT)
                gp.wait_ge(sW, w)

        with nc.Block("pa2_fold") as blk:
            s1 = semN("pa2f_t")

            @blk.tensor
            def _(te):
                for c in range(NCH):
                    for j in range(4):
                        last = te.matmul(pv[0:1, j * 512:(j + 1) * 512], ln1bB[:, c:c + 1],
                                         Wvb[:, c * 2048 + j * 512:c * 2048 + (j + 1) * 512],
                                         start=(c == 0), stop=(c == NCH - 1))
                last.then_inc(s1, 1)

            @blk.vector
            def _(ve):
                ve.wait_ge(s1, 1)
                ve.tensor_add(bvP[:, :], pv[0:1, 0:2048], bvR[:, :])
                for c in range(NCH):
                    ve.tensor_scalar_mul(Wvb[:, c * 2048:(c + 1) * 2048],
                                         Wvb[:, c * 2048:(c + 1) * 2048], ln1w[:, c:c + 1])

        with nc.Block("pa2") as blk:
            stn = semN("pa2_t"); sa = semN("pa2_a"); sSp = semN("pa2_sp")

            @blk.tensor
            def _(te):
                k = 0
                for b_ in range(32):
                    for eh in range(2):
                        pp = k % 2
                        if k >= 2:
                            te.wait_ge(sa, k - 1)
                        for j in range(2):
                            te.matmul(pv[:, pp * 1024 + j * 512:pp * 1024 + (j + 1) * 512],
                                      ones1[:, :],
                                      bvP[:, eh * 1024 + j * 512:eh * 1024 + (j + 1) * 512],
                                      start=True, stop=False)
                        for c in range(NCH):
                            for j in range(2):
                                last = te.matmul(pv[:, pp * 1024 + j * 512:pp * 1024 + (j + 1) * 512],
                                                 z[:, c * T + b_ * 128:c * T + b_ * 128 + 128],
                                                 Wvb[:, c * 2048 + eh * 1024 + j * 512:
                                                     c * 2048 + eh * 1024 + (j + 1) * 512],
                                                 start=False, stop=(c == NCH - 1))
                        last.then_inc(stn, 1)
                        k += 1
                    te.wait_ge(sa, 2 * b_ + 1)
                    for dk in range(2):
                        for j in range(2):
                            te.matmul(plk[:, (dk * 2 + j) * 512:(dk * 2 + j + 1) * 512],
                                      lk[:, b_ * 256 + dk * 128:b_ * 256 + (dk + 1) * 128],
                                      vasm[:, (b_ % 2) * 2048 + j * 512:
                                           (b_ % 2) * 2048 + (j + 1) * 512],
                                      start=(b_ == 0), stop=(b_ == 31))
                te.wait_ge(stn, 64)
                te.nop().then_inc(stn, 1)

            @blk.scalar
            def _(sc):
                k = 0
                for b_ in range(32):
                    for eh in range(2):
                        pp = k % 2
                        sc.wait_ge(stn, b_ * 2 + eh + 1)
                        if b_ >= 2 and eh == 0:
                            sc.wait_ge(sSp, 16 * (b_ - 1))
                        sc.activation(vasm[:, (b_ % 2) * 2048 + eh * 1024:
                                           (b_ % 2) * 2048 + (eh + 1) * 1024],
                                      pv[:, pp * 1024:(pp + 1) * 1024], AF.Silu).then_inc(sa, 1)
                        k += 1

            @blk.sync
            def _(sy):
                for b_ in range(32):
                    sy.wait_ge(sa, b_ * 2 + 2)
                    sy.dma_start(out=vhbm[b_ * 128:(b_ + 1) * 128, :],
                                 in_=vasm[:, (b_ % 2) * 2048:(b_ % 2) * 2048 + 2048]).then_inc(sSp, 16)
                sy.wait_ge(sSp, 16 * 32)

            @blk.vector
            def _(ve):
                ve.wait_ge(stn, 65)
                for dk in range(2):
                    for j in range(2):
                        ve.tensor_single_scalar(linkv[:, dk * 2048 + j * 512:dk * 2048 + (j + 1) * 512],
                                                plk[:, (dk * 2 + j) * 512:(dk * 2 + j + 1) * 512],
                                                1.0 / T, ALU.mult)

        with nc.Block("pa2b") as blk:
            stn = semN("pa2b_t"); sRd = semN("pa2b_rd")
            vrd = vasm

            @blk.sync
            def _(sy):
                for b_ in range(32):
                    if b_ >= 2:
                        sy.wait_ge(stn, b_ - 1)
                    sy.dma_start(out=vrd[:, (b_ % 2) * 2048:(b_ % 2) * 2048 + 1024],
                                 in_=vhbm[b_ * 128:(b_ + 1) * 128, 1024:2048]).then_inc(sRd, 16)

            @blk.tensor
            def _(te):
                for b_ in range(32):
                    te.wait_ge(sRd, 16 * (b_ + 1))
                    for dk in range(2):
                        for j in range(2):
                            last = te.matmul(plk[:, (dk * 2 + j) * 512:(dk * 2 + j + 1) * 512],
                                             lk[:, b_ * 256 + dk * 128:b_ * 256 + (dk + 1) * 128],
                                             vrd[:, (b_ % 2) * 2048 + j * 512:(b_ % 2) * 2048 + (j + 1) * 512],
                                             start=(b_ == 0), stop=(b_ == 31))
                    last.then_inc(stn, 1)

            @blk.vector
            def _(ve):
                ve.wait_ge(stn, 32)
                for dk in range(2):
                    for j in range(2):
                        ve.tensor_single_scalar(linkv[:, dk * 2048 + 1024 + j * 512:
                                                      dk * 2048 + 1024 + (j + 1) * 512],
                                                plk[:, (dk * 2 + j) * 512:(dk * 2 + j + 1) * 512],
                                                1.0 / T, ALU.mult)
        pa2.close()
        lk_g.__exit__(None, None, None)
        dbg_sb("linkv0", linkv[:, 0:2048], 128, 2048)
        dbg_sb("linkv1", linkv[:, 2048:4096], 128, 2048)
        dbg_hbm("vspill", vhbm.rearrange("(n p) k -> n p k", p=128), 32, 128, 2048)

    Vtm_g = None
    if PHASES >= 4:
        Vtm_g = nc.sbuf_tensor("Vtm", [128, 32 * 1024], BF16)
        Vtm = Vtm_g.__enter__().ap()
        pa3 = ExitStack()
        Wfb = pa3.enter_context(nc.sbuf_tensor("Wfb", [128, 8 * 1024], BF16, side="right")).ap()
        xfb = pa3.enter_context(nc.sbuf_tensor("xfb", [128, 8 * 512], BF16, side="right")).ap()
        kst = pa3.enter_context(nc.sbuf_tensor("kst", [128, 2 * 512], BF16, side="right")).ap()
        k2t = pa3.enter_context(nc.sbuf_tensor("k2t", [128, 2 * 512], BF16, side="right")).ap()
        nrm = pa3.enter_context(nc.sbuf_tensor("nrm", [128, 2], F32, side="right")).ap()
        pk = pa3.enter_context(nc.psum_tensor("pk", [128, 2 * 512], F32)).ap()
        pn = pa3.enter_context(nc.psum_tensor("pn", [128, 2 * 512], F32)).ap()
        pvf = pa3.enter_context(nc.psum_tensor("pvf", [128, 2 * 1024], F32)).ap()

        def kq_sweep(tag, wdram, bcol, hbm_out, qmax_store):
            nblocks = 4 if qmax_store else 8
            with nc.Block(tag + "_load") as blk:
                @blk.gpsimd
                def _(gp):
                    w = wload(gp, Wfb[:, :], wdram)
                    gp.wait_ge(sW, w)

            with nc.Block(tag) as blk:
                stn = semN(tag + "_t"); sa = semN(tag + "_a"); sv = semN(tag + "_v")
                sSp = semN(tag + "_sp")
                vdone = lambda kk: (kk // 8 + 1) + kk + 1
                kq_seq = []
                for kk in range(nblocks * 8):
                    kq_seq.append(("K", kk))
                    if kk % 8 >= 1:
                        kq_seq.append(("N", kk - 1))
                    if kk % 8 == 7:
                        kq_seq.append(("N", kk))
                kpos = {}
                npos = {}
                for i, (t_, k_) in enumerate(kq_seq):
                    (kpos if t_ == "K" else npos)[k_] = i + 1

                @blk.vector
                def _(ve):
                    for t5 in range(nblocks):
                        mch = MCH512[t5]
                        if t5 > 0:
                            ve.wait_ge(stn, kpos[t5 * 8 - 1])
                        for c in range(mch):
                            ve.tensor_scalar(xfb[:, c * 512:(c + 1) * 512],
                                             z[:, c * T + t5 * 512:c * T + (t5 + 1) * 512],
                                             lnfw[:, c:c + 1], lnfb[:, c:c + 1],
                                             ALU.mult, ALU.add)
                        ve.engine_nop().then_inc(sv, 1)
                        for dk in range(8):
                            kk = t5 * 8 + dk
                            pp = kk % 2
                            ve.wait_ge(stn, npos[kk])
                            ve.reduce_max(nrm[:, pp:pp + 1],
                                          pn[:, pp * 512:(pp + 1) * 512], AXX)
                            if qmax_store:
                                ve.tensor_copy(QMax2[:, dk * 4 + t5:dk * 4 + t5 + 1],
                                               nrm[:, pp:pp + 1])
                            else:
                                if t5 == 0:
                                    ve.tensor_copy(KMax2[:, dk:dk + 1], nrm[:, pp:pp + 1])
                                else:
                                    ve.tensor_max(KMax2[:, dk:dk + 1], KMax2[:, dk:dk + 1],
                                                  nrm[:, pp:pp + 1])
                            ve.engine_nop().then_inc(sv, 1)

                @blk.tensor
                def _(te):
                    for t_, kk in kq_seq:
                        t5 = kk // 8
                        mch = MCH512[t5]
                        pp = kk % 2
                        if t_ == "K":
                            if kk % 8 == 0:
                                te.wait_ge(sv, t5 * 9 + 1)
                            if kk >= 2:
                                te.wait_ge(sa, kk - 1)
                            for c in range(mch):
                                last = te.matmul(pk[:, pp * 512:(pp + 1) * 512],
                                                 Wfb[:, c * 1024 + (kk % 8) * 128:
                                                     c * 1024 + (kk % 8 + 1) * 128],
                                                 xfb[:, c * 512:(c + 1) * 512],
                                                 start=(c == 0), stop=(c == mch - 1))
                            last.then_inc(stn, 1)
                        else:
                            te.wait_ge(sa, kk + 1)
                            if kk >= 2:
                                te.wait_ge(sv, vdone(kk - 2))
                            te.matmul(pn[:, pp * 512:(pp + 1) * 512], ones128[:, :],
                                      k2t[:, pp * 512:(pp + 1) * 512],
                                      start=True, stop=True).then_inc(stn, 1)

                @blk.scalar
                def _(sc):
                    for t5 in range(nblocks):
                        for dk in range(8):
                            kk = t5 * 8 + dk
                            pp = kk % 2
                            sc.wait_ge(stn, kpos[kk])
                            if kk >= 2:
                                sc.wait_ge(sSp, 16 * (kk - 1))
                            sc.activation(kst[:, pp * 512:(pp + 1) * 512],
                                          pk[:, pp * 512:(pp + 1) * 512], AF.Identity,
                                          bias=bcol[:, dk:dk + 1])
                            sc.activation(k2t[:, pp * 512:(pp + 1) * 512],
                                          kst[:, pp * 512:(pp + 1) * 512],
                                          AF.Square).then_inc(sa, 1)

                @blk.sync
                def _(sy):
                    for t5 in range(nblocks):
                        for dk in range(8):
                            kk = t5 * 8 + dk
                            pp = kk % 2
                            sy.wait_ge(sa, kk + 1)
                            sy.dma_start(out=hbm_out[dk, :, t5 * 512:(t5 + 1) * 512],
                                         in_=kst[:, pp * 512:(pp + 1) * 512]).then_inc(sSp, 16)
                    sy.wait_ge(sSp, 16 * nblocks * 8)

        kq_sweep("pa3k", kwT, kbC, khbm, False)
        kq_sweep("pa3q", qwT, qbC, qhbm, True)

        with nc.Block("pa3nb") as blk:
            @blk.vector
            def _(ve):
                for hh in range(8):
                    for qb in range(4):
                        ve.tensor_mul(negB[:, hh * 4 + qb:hh * 4 + qb + 1],
                                      QMax2[:, hh * 4 + qb:hh * 4 + qb + 1],
                                      KMax2[:, hh:hh + 1])

        with nc.Block("pa3nb2") as blk:
            @blk.scalar
            def _(sc):
                sc.activation(negB[:, :], negB[:, :], AF.Sqrt)
                sc.mul(negB[:, :], negB[:, :], -1.0 / 32.0)

        with nc.Block("pa3v_load") as blk:
            @blk.gpsimd
            def _(gp):
                w = wload(gp, Wfb[:, :], vwT)
                gp.wait_ge(sW, w)

        with nc.Block("pa3v") as blk:
            stn = semN("pa3v_t"); sv = semN("pa3v_v")
            vcopy = lambda b: (b // 4 + 1) + b + 1

            @blk.vector
            def _(ve):
                for t5 in range(8):
                    mch = MCH512[t5]
                    if t5 > 0:
                        ve.wait_ge(stn, 4 * t5)
                    for c in range(mch):
                        ve.tensor_scalar(xfb[:, c * 512:(c + 1) * 512],
                                         z[:, c * T + t5 * 512:c * T + (t5 + 1) * 512],
                                         lnfw[:, c:c + 1], lnfb[:, c:c + 1],
                                         ALU.mult, ALU.add)
                    ve.engine_nop().then_inc(sv, 1)
                    for tb in range(4):
                        b_ = t5 * 4 + tb
                        ve.wait_ge(stn, b_ + 1)
                        ve.tensor_copy(Vtm[:, b_ * 1024:(b_ + 1) * 1024],
                                       pvf[:, (b_ % 2) * 1024:(b_ % 2) * 1024 + 1024])
                        ve.engine_nop().then_inc(sv, 1)

            @blk.tensor
            def _(te):
                for t5 in range(8):
                    mch = MCH512[t5]
                    te.wait_ge(sv, t5 * 5 + 1)
                    for tb in range(4):
                        b_ = t5 * 4 + tb
                        pp = b_ % 2
                        if b_ >= 2:
                            te.wait_ge(sv, vcopy(b_ - 2))
                        for j in range(2):
                            te.matmul(pvf[:, pp * 1024 + j * 512:pp * 1024 + (j + 1) * 512],
                                      ones1[:, :], vbR[:, j * 512:(j + 1) * 512],
                                      start=True, stop=False)
                        for c in range(mch):
                            for j in range(2):
                                last = te.matmul(pvf[:, pp * 1024 + j * 512:pp * 1024 + (j + 1) * 512],
                                                 xfb[:, c * 512 + tb * 128:c * 512 + tb * 128 + 128],
                                                 Wfb[:, c * 1024 + j * 512:c * 1024 + (j + 1) * 512],
                                                 start=False, stop=(c == mch - 1))
                        last.then_inc(stn, 1)
        pa3.close()
        dbg_hbm("Kf", khbm, 8, 128, T)
        dbg_hbm("Qf", qhbm, 8, 128, OWN)
        dbg_sb("negB", negB[:, :], 128, 32)
        for b_ in (0, 17):
            dbg_sb("Vtm%d" % b_, Vtm[:, b_ * 1024:(b_ + 1) * 1024], 128, 1024)

    if PHASES >= 5:
        pbs = ExitStack()
        Khb = pbs.enter_context(nc.sbuf_tensor("Khb", [128, 2 * T], BF16, side="right")).ap()
        Qrd = pbs.enter_context(nc.sbuf_tensor("Qrd", [128, 2 * 512], BF16, side="right")).ap()
        PT = pbs.enter_context(nc.sbuf_tensor("PT", [128, 2 * 1536], BF16, side="right")).ap()
        sums = pbs.enter_context(nc.sbuf_tensor("sums", [1, 512], F32, side="right")).ap()
        rsum = pbs.enter_context(nc.sbuf_tensor("rsum", [128, 512], F32, side="right")).ap()
        atile = pbs.enter_context(nc.sbuf_tensor("atile", [128, 2 * 512], BF16, side="right")).ap()
        pS = pbs.enter_context(nc.psum_tensor("pS", [128, 2 * 1536], F32)).ap()
        pPV = pbs.enter_context(nc.psum_tensor("pPV", [128, 512], F32)).ap()
        pSm = pbs.enter_context(nc.psum_tensor("pSm", [1, 512], F32)).ap()

        GROUPS = [3] * 10 + [2]
        NG = len(GROUPS)

        issue = []
        issue.append(("K", 0))
        for hh in range(8):
            for qb in range(4):
                it = hh * 4 + qb
                issue.append(("Q", it))
                if qb == 0 and hh < 7:
                    issue.append(("K", hh + 1))
        posK = {hh: issue.index(("K", hh)) + 1 for hh in range(8)}
        posQ = {it: issue.index(("Q", it)) + 1 for it in range(32)}

        with nc.Block("pb") as blk:
            stn = semN("pb_t"); sa = semN("pb_a"); sv = semN("pb_v")
            sRd = semN("pb_rd"); sSp = semN("pb_sp")

            @blk.sync
            def _(sy):
                for ent in issue:
                    if ent[0] == "K":
                        hh = ent[1]
                        sy.dma_start(out=Khb[:, (hh % 2) * T:(hh % 2) * T + T],
                                     in_=khbm[hh, :, :]).then_inc(sRd, 16)
                    else:
                        it = ent[1]
                        if it >= 2:
                            sy.wait_ge(sv, 2 * it - 2)
                        sy.dma_start(out=Qrd[:, (it % 2) * 512:(it % 2) * 512 + 512],
                                     in_=qhbm[it // 4, :, (it % 4) * 512:(it % 4) * 512 + 512]
                                     ).then_inc(sRd, 16)
                        if it >= 1:
                            sy.wait_ge(sv, 2 * it)
                            sy.dma_start(out=attnhbm[(it - 1) // 4, :,
                                                     ((it - 1) % 4) * 512:((it - 1) % 4) * 512 + 512],
                                         in_=atile[:, ((it - 1) % 2) * 512:((it - 1) % 2) * 512 + 512]
                                         ).then_inc(sSp, 16)
                sy.wait_ge(sv, 64)
                sy.dma_start(out=attnhbm[7, :, 3 * 512:4 * 512],
                             in_=atile[:, 1 * 512:2 * 512]).then_inc(sSp, 16)
                sy.wait_ge(sSp, 16 * 32)

            @blk.tensor
            def _(te):
                for hh in range(8):
                    for qb in range(4):
                        it = hh * 4 + qb
                        tbase = it * (2 * NG + 1)
                        abase = it * NG
                        te.wait_ge(sRd, 16 * max(posK[hh], posQ[it]))
                        if it >= 1:
                            te.wait_ge(sv, 2 * it)
                        ktof = [0]
                        for sz in GROUPS:
                            ktof.append(ktof[-1] + sz)

                        def emit_S(g):
                            pp = g % 2
                            te.wait_ge(sa, abase + max(g - 1, 0))
                            sz = GROUPS[g]
                            last = None
                            for j in range(sz):
                                last = te.matmul(pS[:, pp * 1536 + j * 512:pp * 1536 + (j + 1) * 512],
                                                 Khb[:, (hh % 2) * T + (ktof[g] + j) * 128:
                                                     (hh % 2) * T + (ktof[g] + j + 1) * 128],
                                                 Qrd[:, (it % 2) * 512:(it % 2) * 512 + 512],
                                                 start=True, stop=True)
                            last.then_inc(stn, 1)

                        def emit_PV(g):
                            pp = g % 2
                            te.wait_ge(sa, abase + g + 1)
                            sz = GROUPS[g]
                            last = None
                            for j in range(sz):
                                ktj = ktof[g] + j
                                te.matmul(pPV[:, :],
                                          Vtm[:, ktj * 1024 + hh * 128:ktj * 1024 + (hh + 1) * 128],
                                          PT[:, pp * 1536 + j * 512:pp * 1536 + (j + 1) * 512],
                                          start=(ktj == 0), stop=(ktj == 31))
                                last = te.matmul(pSm[:, :], onescol[:, :],
                                                 PT[:, pp * 1536 + j * 512:pp * 1536 + (j + 1) * 512],
                                                 start=(ktj == 0), stop=(ktj == 31))
                            last.then_inc(stn, 1)

                        emit_S(0)
                        for g in range(1, NG):
                            emit_S(g)
                            emit_PV(g - 1)
                        emit_PV(NG - 1)
                        te.wait_ge(sv, 2 * it + 1)
                        te.matmul(pS[:, 0:512], ones1f[:, :], sums[:, :],
                                  start=True, stop=True).then_inc(stn, 1)

            @blk.scalar
            def _(sc):
                for it in range(32):
                    tbase = it * (2 * NG + 1)
                    abase = it * NG
                    for g, sz in enumerate(GROUPS):
                        pp = g % 2
                        sc.wait_ge(stn, tbase + (1 if g == 0 else 2 * g))
                        sc.activation(PT[:, pp * 1536:pp * 1536 + sz * 512],
                                      pS[:, pp * 1536:pp * 1536 + sz * 512],
                                      AF.Exp, bias=negB[:, it:it + 1],
                                      scale=1.0 / 32.0).then_inc(sa, 1)

            @blk.vector
            def _(ve):
                for it in range(32):
                    tbase = it * (2 * NG + 1)
                    ve.wait_ge(stn, tbase + 2 * NG)
                    ve.tensor_copy(sums[:, :], pSm[:, :]).then_inc(sv, 1)
                    ve.wait_ge(stn, tbase + 2 * NG + 1)
                    if it >= 2:
                        ve.wait_ge(sSp, 16 * (it - 1))
                    ve.reciprocal(rsum[:, :], pS[:, 0:512])
                    ve.tensor_mul(atile[:, (it % 2) * 512:(it % 2) * 512 + 512],
                                  pPV[:, :], rsum[:, :])
                    ve.engine_nop().then_inc(sv, 1)
        pbs.close()
        Vtm_g.__exit__(None, None, None)
        dbg_hbm("attn", attnhbm, 8, 128, OWN)

    if PHASES >= 6:
        gatehbm = nc.dram_tensor("gatehbm", [OWN, 2048], BF16, kind="Internal").ap()
        pc = ExitStack()
        zo = pc.enter_context(nc.sbuf_tensor("zo", [128, 8 * OWN], BF16, side="right")).ap()
        lq = pc.enter_context(nc.sbuf_tensor("lq", [128, 2 * OWN], BF16, side="right")).ap()
        P2 = pc.enter_context(nc.sbuf_tensor("P2", [128, 16 * 128], BF16, side="right")).ap()
        ptmp = pc.enter_context(nc.sbuf_tensor("ptmp", [128, 128], F32, side="right")).ap()

        with nc.Block("pc_z") as blk:
            @blk.vector
            def _(ve):
                for c in range(NCH):
                    ve.tensor_copy(zo[:, c * OWN:(c + 1) * OWN], z[:, c * T:c * T + OWN])
        z_g.__exit__(None, None, None)

        pcH = ExitStack()
        qkF = pcH.enter_context(nc.sbuf_tensor("qkF", [128, 2 * OWN], BF16, side="right")).ap()
        qq = pcH.enter_context(nc.sbuf_tensor("qq", [128, 2 * OWN], BF16, side="right")).ap()
        qkk = pcH.enter_context(nc.sbuf_tensor("qkk", [128, 2 * OWN], BF16, side="right")).ap()
        hpre = pcH.enter_context(nc.sbuf_tensor("hpre", [128, 2 * OWN], BF16, side="right")).ap()

        with nc.Block("pc_qk") as blk:
            pq = ExitStack()
            Wqkb2 = pq.enter_context(nc.sbuf_tensor("Wqkb2", [128, 8 * 256], BF16, side="right")).ap()
            pqk2 = pq.enter_context(nc.psum_tensor("pqk2", [128, 2 * 512], F32)).ap()
            stn = semN("pcqk_t"); sa = semN("pcqk_a")

            sfold = semN("pcqk_f")

            @blk.gpsimd
            def _(gp):
                wload(gp, Wqkb2[:, :], WqkT)

            @blk.vector
            def _(ve):
                ve.wait_ge(sW, wctr[0])
                for c in range(NCH):
                    ve.tensor_scalar_mul(Wqkb2[:, c * 256:(c + 1) * 256],
                                         Wqkb2[:, c * 256:(c + 1) * 256], ln1w[:, c:c + 1])
                ve.engine_nop().then_inc(sfold, 1)

            @blk.tensor
            def _(te):
                te.wait_ge(sfold, 1)
                k = 0
                for dk in range(2):
                    for t5 in range(4):
                        pp = k % 2
                        if k >= 2:
                            te.wait_ge(sa, k - 1)
                        te.matmul(pqk2[:, pp * 512:(pp + 1) * 512],
                                  bqkP[0:1, dk * 128:(dk + 1) * 128],
                                  onesRow[:, :], start=True, stop=False)
                        for c in range(NCH):
                            last = te.matmul(pqk2[:, pp * 512:(pp + 1) * 512],
                                             Wqkb2[:, c * 256 + dk * 128:c * 256 + (dk + 1) * 128],
                                             zo[:, c * OWN + t5 * 512:c * OWN + (t5 + 1) * 512],
                                             start=False, stop=(c == NCH - 1))
                        last.then_inc(stn, 1)
                        k += 1

            @blk.scalar
            def _(sc):
                k = 0
                for dk in range(2):
                    for t5 in range(4):
                        pp = k % 2
                        sc.wait_ge(stn, k + 1)
                        sc.activation(qkF[:, dk * OWN + t5 * 512:dk * OWN + (t5 + 1) * 512],
                                      pqk2[:, pp * 512:(pp + 1) * 512], AF.Silu).then_inc(sa, 1)
                        k += 1
            pq.close()

        with nc.Block("pc_rope") as blk:
            @blk.vector
            def _(ve):
                for i, dst in [(0, qq), (1, lq), (2, qkk)]:
                    for c in range(2):
                        ve.tensor_scalar(hpre[:, c * OWN:(c + 1) * OWN],
                                         qkF[:, c * OWN:(c + 1) * OWN],
                                         gamC[:, i * 2 + c:i * 2 + c + 1],
                                         betC[:, i * 2 + c:i * 2 + c + 1],
                                         ALU.mult, ALU.add)
                    x1 = hpre[:, 0:OWN]; x2_ = hpre[:, OWN:2 * OWN]
                    co = cosF[:, 0:OWN]; si = sinF[:, 0:OWN]
                    o1 = dst[:, 0:OWN]; o2 = dst[:, OWN:2 * OWN]
                    ve.tensor_mul(o1, x1, co)
                    ve.tensor_mul(o2, x2_, si)
                    ve.tensor_sub(o1, o1, o2)
                    ve.tensor_mul(o2, x2_, co)
                    ve.tensor_mul(x1, x1, si)
                    ve.tensor_add(o2, o2, x1)

        with nc.Block("pc_quad") as blk:
            pq = ExitStack()
            pqd = pq.enter_context(nc.psum_tensor("pqd", [128, 2 * 512], F32)).ap()
            stn = semN("pcq_t"); sa = semN("pcq_a"); sv = semN("pcq_v")

            @blk.gpsimd
            def _(gp):
                gp.memset(P2[:, :], 0.0).then_inc(stn, 1)

            @blk.tensor
            def _(te):
                te.wait_ge(stn, 1)
                for g in range(32):
                    base = (g % 2) * 64
                    pp = g % 2
                    if g >= 2:
                        te.wait_ge(sv, g - 1)
                    for c in range(2):
                        last = te.matmul(pqd[base:base + 64, pp * 512:pp * 512 + 64],
                                         qkk[:, c * OWN + g * 64:c * OWN + (g + 1) * 64],
                                         qq[:, c * OWN + g * 64:c * OWN + (g + 1) * 64],
                                         start=(c == 0), stop=(c == 1),
                                         tile_position=(0, base))
                    last.then_inc(stn, 1)

            @blk.vector
            def _(ve):
                for g in range(32):
                    base = (g % 2) * 64
                    pp = g % 2
                    tb = g // 2
                    ve.wait_ge(stn, g + 2)
                    ve.scalar_tensor_tensor(ptmp[base:base + 64, 0:64],
                                            pqd[base:base + 64, pp * 512:pp * 512 + 64],
                                            1.0 / 64.0, qbias[base:base + 64, :],
                                            ALU.mult, ALU.add)
                    ve.tensor_relu(ptmp[base:base + 64, 64:128], ptmp[base:base + 64, 0:64])
                    ve.tensor_mul(P2[base:base + 64, tb * 128 + base:tb * 128 + base + 64],
                                  ptmp[base:base + 64, 0:64],
                                  ptmp[base:base + 64, 64:128]).then_inc(sv, 1)
            pq.close()
        dbg_sb("qkF", qkF[:, :], 128, 2 * OWN)
        dbg_sb("qq", qq[:, :], 128, 2 * OWN)
        dbg_sb("lqd", lq[:, :], 128, 2 * OWN)
        dbg_sb("qkk", qkk[:, :], 128, 2 * OWN)
        dbg_sb("P2", P2[:, :], 128, 16 * 128)
        pcH.close()
        trF_g.__exit__(None, None, None)

        pcg = ExitStack()
        Wgb = pcg.enter_context(nc.sbuf_tensor("Wgb", [128, 8 * 2048], BF16, side="right")).ap()
        gstg = pcg.enter_context(nc.sbuf_tensor("gstg", [128, 2 * 2048], BF16, side="right")).ap()
        bgP = pcg.enter_context(nc.sbuf_tensor("bgP", [1, 2048], BF16, side="right")).ap()
        pgg = pcg.enter_context(nc.psum_tensor("pgg", [128, 2 * 2048], F32)).ap()

        with nc.Block("pc_gload") as blk:
            @blk.gpsimd
            def _(gp):
                w = wload(gp, Wgb[:, :], WgT)
                gp.wait_ge(sW, w)

        with nc.Block("pc_gfold") as blk:
            s1 = semN("pcgf_t")

            @blk.tensor
            def _(te):
                for c in range(NCH):
                    for j in range(4):
                        last = te.matmul(pgg[0:1, j * 512:(j + 1) * 512], ln1bB[:, c:c + 1],
                                         Wgb[:, c * 2048 + j * 512:c * 2048 + (j + 1) * 512],
                                         start=(c == 0), stop=(c == NCH - 1))
                last.then_inc(s1, 1)

            @blk.vector
            def _(ve):
                ve.wait_ge(s1, 1)
                ve.tensor_add(bgP[:, :], pgg[0:1, 0:2048], bgR[:, :])
                for c in range(NCH):
                    ve.tensor_scalar_mul(Wgb[:, c * 2048:(c + 1) * 2048],
                                         Wgb[:, c * 2048:(c + 1) * 2048], ln1w[:, c:c + 1])

        with nc.Block("pc_gate") as blk:
            stn = semN("pcg_t"); sa = semN("pcg_a"); sSp = semN("pcg_sp")

            @blk.tensor
            def _(te):
                for b_ in range(16):
                    pp = b_ % 2
                    if b_ >= 2:
                        te.wait_ge(sa, b_ - 1)
                    for j in range(4):
                        te.matmul(pgg[:, pp * 2048 + j * 512:pp * 2048 + (j + 1) * 512],
                                  ones1[:, :], bgP[:, j * 512:(j + 1) * 512],
                                  start=True, stop=False)
                    for c in range(NCH):
                        for j in range(4):
                            last = te.matmul(pgg[:, pp * 2048 + j * 512:pp * 2048 + (j + 1) * 512],
                                             zo[:, c * OWN + b_ * 128:c * OWN + b_ * 128 + 128],
                                             Wgb[:, c * 2048 + j * 512:c * 2048 + (j + 1) * 512],
                                             start=False, stop=(c == NCH - 1))
                    last.then_inc(stn, 1)

            @blk.scalar
            def _(sc):
                for b_ in range(16):
                    pp = b_ % 2
                    sc.wait_ge(stn, b_ + 1)
                    if b_ >= 2:
                        sc.wait_ge(sSp, 16 * (b_ - 1))
                    sc.activation(gstg[:, pp * 2048:(pp + 1) * 2048],
                                  pgg[:, pp * 2048:(pp + 1) * 2048], AF.Silu).then_inc(sa, 1)

            @blk.sync
            def _(sy):
                for b_ in range(16):
                    sy.wait_ge(sa, b_ + 1)
                    sy.dma_start(out=gatehbm[b_ * 128:(b_ + 1) * 128, :],
                                 in_=gstg[:, (b_ % 2) * 2048:(b_ % 2) * 2048 + 2048]).then_inc(sSp, 16)
                sy.wait_ge(sSp, 16 * 16)
        pcg.close()
        dbg_hbm("gate", gatehbm.rearrange("(n p) k -> n p k", p=128), 16, 128, 2048)

        pcm = ExitStack()
        Wob = pcm.enter_context(nc.sbuf_tensor("Wob", [128, 16 * 1024], BF16, side="right")).ap()
        vrd2 = pcm.enter_context(nc.sbuf_tensor("vrd2", [128, 2 * 2048], BF16, side="right")).ap()
        grd2 = pcm.enter_context(nc.sbuf_tensor("grd2", [128, 2 * 2048], BF16, side="right")).ap()
        gtile = pcm.enter_context(nc.sbuf_tensor("gtile", [128, 4 * 2048], F32, side="right")).ap()
        gatF = pcm.enter_context(nc.sbuf_tensor("gatF", [128, 16 * 512], BF16, side="right")).ap()
        ogsb = pcm.enter_context(nc.sbuf_tensor("ogsb", [128, 8 * 512], BF16, side="right")).ap()
        og2 = pcm.enter_context(nc.sbuf_tensor("og2", [128, 2 * 512], BF16, side="right")).ap()
        st8 = pcm.enter_context(nc.sbuf_tensor("st8", [128, 6 * 512], F32, side="right")).ap()
        PZ = pcm.enter_context(nc.psum_tensor("PZ", [128, 4096], F32)).ap()

        def pqp_slot(tb):
            return PZ[:, (tb % 2) * 2048:(tb % 2) * 2048 + 2048]
        pogf = PZ[:, 2560:3584]
        pstA = PZ[:, 3584:4096]
        pstB = PZ[:, 2048:2560]

        def ptr_slot(kk):
            off = 3584 if (kk // 4) % 2 == 0 else 2048
            return PZ[:, off + (kk % 4) * 128:off + (kk % 4) * 128 + 128]

        with nc.Block("pc_wload") as blk:
            @blk.gpsimd
            def _(gp):
                w = wload(gp, Wob[:, :], WoT)
                gp.wait_ge(sW, w)

        stn = semN("pcm_t"); sa = semN("pcm_a")
        sv = semN("pcm_v"); sv2 = semN("pcm_v2")
        sRd = semN("pcm_rd"); sSp = semN("pcm_sp")
        for sup in range(SUPS):
            Bt = sup * 36; Ba = sup * 9; Bv = sup * 4; Bv2 = sup * 18
            Br = sup * 16 * 8; Bs = sup * 16 * 8
            with nc.Block("pc_m%d" % sup) as blk:

                @blk.sync
                def _(sy):
                    for tb in range(4):
                        b_ = sup * 4 + tb
                        if tb >= 2:
                            sy.wait_ge(stn, Bt + tb - 1)
                            sy.wait_ge(sv, Bv + tb - 1)
                        sy.dma_start(out=vrd2[:, (tb % 2) * 2048:(tb % 2) * 2048 + 2048],
                                     in_=vhbm[b_ * 128:(b_ + 1) * 128, :]).then_inc(sRd, 16)
                        sy.dma_start(out=grd2[:, (tb % 2) * 2048:(tb % 2) * 2048 + 2048],
                                     in_=gatehbm[b_ * 128:(b_ + 1) * 128, :]).then_inc(sRd, 16)
                    sy.wait_ge(sv2, Bv2 + 18)
                    for do in range(8):
                        sy.dma_start(out=gauhbm[do, :, sup * 512:(sup + 1) * 512],
                                     in_=ogsb[:, do * 512:(do + 1) * 512]).then_inc(sSp, 16)
                    sy.wait_ge(sSp, Bs + 16 * 8)

                @blk.tensor
                def _(te):
                    for tb in range(4):
                        b_ = sup * 4 + tb
                        if tb >= 2:
                            te.wait_ge(sv, Bv + tb - 1)
                        te.wait_ge(sRd, Br + 16 * (2 * tb + 1))
                        pq_ = pqp_slot(tb)
                        for j in range(4):
                            for dk in range(2):
                                te.matmul(pq_[:, j * 512:(j + 1) * 512],
                                          lq[:, dk * OWN + b_ * 128:dk * OWN + b_ * 128 + 128],
                                          linkv[:, dk * 2048 + j * 512:dk * 2048 + (j + 1) * 512],
                                          start=(dk == 0), stop=False)
                            last = te.matmul(pq_[:, j * 512:(j + 1) * 512],
                                             P2[:, (b_ % 16) * 128:((b_ % 16) + 1) * 128],
                                             vrd2[:, (tb % 2) * 2048 + j * 512:(tb % 2) * 2048 + (j + 1) * 512],
                                             start=False, stop=True)
                        last.then_inc(stn, 1)
                    te.wait_ge(sv, Bv + 4)
                    for grp in range(16):
                        tb, ec0 = grp // 4, (grp % 4) * 4
                        if grp >= 2:
                            te.wait_ge(sv2, Bv2 + grp - 1)
                        for q_ in range(4):
                            kk = grp * 4 + q_
                            last = te.transpose(ptr_slot(kk),
                                                gtile[:, tb * 2048 + (ec0 + q_) * 128:
                                                      tb * 2048 + (ec0 + q_ + 1) * 128],
                                                identF[:, :])
                        last.then_inc(stn, 1)
                    te.wait_ge(sv2, Bv2 + 16)
                    for do in range(8):
                        if do >= 2:
                            te.wait_ge(sa, Ba + do - 1)
                        for ec in range(16):
                            last = te.matmul(pogf[:, (do % 2) * 512:(do % 2) * 512 + 512],
                                             Wob[:, ec * 1024 + do * 128:ec * 1024 + (do + 1) * 128],
                                             gatF[:, ec * 512:(ec + 1) * 512],
                                             start=(ec == 0), stop=(ec == 15))
                        last.then_inc(stn, 1)
                        te.wait_ge(sa, Ba + do + 1)
                        te.matmul(pstA[:, 0:512], ones128[:, :], ogsb[:, do * 512:(do + 1) * 512],
                                  start=(do == 0), stop=(do == 7))
                        last = te.matmul(pstB[:, 0:512], ones128[:, :], og2[:, (do % 2) * 512:(do % 2) * 512 + 512],
                                         start=(do == 0), stop=(do == 7))
                        last.then_inc(stn, 1)

                @blk.scalar
                def _(sc):
                    for do in range(8):
                        sc.wait_ge(stn, Bt + 4 + 16 + 2 * do + 1)
                        sc.activation(ogsb[:, do * 512:(do + 1) * 512],
                                      pogf[:, (do % 2) * 512:(do % 2) * 512 + 512],
                                      AF.Identity, bias=boC[:, do:do + 1])
                        if do >= 2:
                            sc.wait_ge(stn, Bt + 4 + 16 + 2 * (do - 1))
                        sc.activation(og2[:, (do % 2) * 512:(do % 2) * 512 + 512],
                                      ogsb[:, do * 512:(do + 1) * 512], AF.Square).then_inc(sa, 1)
                    sc.wait_ge(sv2, Bv2 + 17)
                    sc.activation(st8[:, 2 * 512:3 * 512], st8[:, 1 * 512:2 * 512], AF.Sqrt).then_inc(sa, 1)

                @blk.vector
                def _(ve):
                    for tb in range(4):
                        ve.wait_ge(stn, Bt + tb + 1)
                        ve.wait_ge(sRd, Br + 16 * (2 * tb + 2))
                        ve.tensor_tensor(gtile[:, tb * 2048:(tb + 1) * 2048], pqp_slot(tb),
                                         grd2[:, (tb % 2) * 2048:(tb % 2) * 2048 + 2048],
                                         ALU.mult)
                        ve.engine_nop().then_inc(sv, 1)
                    for grp in range(16):
                        tb, ec0 = grp // 4, (grp % 4) * 4
                        base = 3584 if grp % 2 == 0 else 2048
                        ve.wait_ge(stn, Bt + 4 + grp + 1)
                        ve.tensor_copy(
                            gatF.rearrange("p (e r) -> p e r", e=16)[
                                :, ec0:ec0 + 4, tb * 128:(tb + 1) * 128],
                            PZ[:, base:base + 512].rearrange("p (e r) -> p e r", e=4)
                        ).then_inc(sv2, 1)
                    ve.wait_ge(stn, Bt + 4 + 16 + 16)
                    sl = [st8[:, j * 512:(j + 1) * 512] for j in range(6)]
                    ve.tensor_single_scalar(sl[0], pstA[:, 0:512], 1.0 / D, ALU.mult)
                    ve.tensor_single_scalar(sl[1], pstB[:, 0:512], 1.0 / D, ALU.mult)
                    ve.tensor_tensor(sl[3], sl[0], sl[0], ALU.mult)
                    ve.tensor_sub(sl[1], sl[1], sl[3])
                    ve.tensor_single_scalar(sl[1], sl[1], EPS, ALU.add)
                    ve.engine_nop()
                    ve.engine_nop().then_inc(sv2, 1)
                    ve.wait_ge(sa, Ba + 8 + 1)
                    ve.reciprocal(sl[3], sl[2])
                    ve.tensor_tensor(sl[5], sl[3], sl[3], ALU.mult)
                    ve.tensor_tensor(sl[5], sl[5], sl[1], ALU.mult)
                    ve.tensor_scalar(sl[5], sl[5], -0.5, 1.5, ALU.mult, ALU.add)
                    ve.tensor_tensor(sl[3], sl[3], sl[5], ALU.mult)
                    for do in range(8):
                        o_ = ogsb[:, do * 512:(do + 1) * 512]
                        ve.tensor_sub(sl[4], o_, sl[0])
                        ve.tensor_tensor(sl[4], sl[4], sl[3], ALU.mult)
                        ve.tensor_scalar(o_, sl[4], ln2w[:, do:do + 1], ln2b[:, do:do + 1],
                                         ALU.mult, ALU.add)
                    ve.engine_nop().then_inc(sv2, 1)
        for _tb in range(4):
            dbg_sb("gtile%d" % _tb, gtile[:, _tb * 2048:(_tb + 1) * 2048], 128, 2048)
        for _ec in (0, 7, 15):
            dbg_sb("gatF%d" % _ec, gatF[:, _ec * 512:(_ec + 1) * 512], 128, 512)
        pcm.close()
        pc.close()
        dbg_hbm("gau", gauhbm, 8, 128, OWN)

    if PHASES >= 7:
        pd = ExitStack()
        Owb = pd.enter_context(nc.sbuf_tensor("Owb", [128, 8 * 1024], BF16, side="right")).ap()
        ard = pd.enter_context(nc.sbuf_tensor("ard", [128, 8 * 512], BF16, side="right")).ap()
        grd = pd.enter_context(nc.sbuf_tensor("grd", [128, 8 * 512], BF16, side="right")).ap()
        ogf = pd.enter_context(nc.sbuf_tensor("ogf", [128, 8 * 512], BF16, side="right")).ap()
        xrd = pd.enter_context(nc.sbuf_tensor("xrd", [128, 2 * 512], F32, side="right")).ap()
        fout = pd.enter_context(nc.sbuf_tensor("fout", [128, 2 * 512], F32, side="right")).ap()
        ftmp = pd.enter_context(nc.sbuf_tensor("ftmp", [128, 512], F32, side="right")).ap()
        pof = pd.enter_context(nc.psum_tensor("pof", [128, 2 * 512], F32)).ap()

        with nc.Block("pd_load") as blk:
            @blk.gpsimd
            def _(gp):
                w = wload(gp, Owb[:, :], owT)
                gp.wait_ge(sW, w)

        stn = semN("pd_t"); sv = semN("pd_v"); sa = semN("pd_a")
        sRd = semN("pd_rd"); sO = semN("pd_o")
        Bt = Ba = Bv = Br = Bo = 0
        for e5 in range(4):
            mch = MCH512[e5]
            with nc.Block("pd_%d" % e5) as blk:

                @blk.sync
                def _(sy):
                    for mc in range(mch):
                        sy.dma_start(out=ard[:, mc * 512:(mc + 1) * 512],
                                     in_=attnhbm[mc, :, e5 * 512:(e5 + 1) * 512]).then_inc(sRd, 16)
                    for do in range(8):
                        sy.dma_start(out=grd[:, do * 512:(do + 1) * 512],
                                     in_=gauhbm[do, :, e5 * 512:(e5 + 1) * 512]).then_inc(sRd, 16)
                    for dc in range(8):
                        if dc >= 2:
                            sy.wait_ge(sv, Bv + dc - 1)
                        sy.dma_start(out=xrd[:, (dc % 2) * 512:(dc % 2) * 512 + 512],
                                     in_=xT[dc * 128:(dc + 1) * 128, e5 * 512:(e5 + 1) * 512]
                                     ).then_inc(sRd, 16)
                        sy.wait_ge(sv, Bv + dc + 1)
                        sy.dma_start(out=outT[dc * 128:(dc + 1) * 128, e5 * 512:(e5 + 1) * 512],
                                     in_=fout[:, (dc % 2) * 512:(dc % 2) * 512 + 512]).then_inc(sO, 16)
                    sy.wait_ge(sO, Bo + 16 * 8)

                @blk.tensor
                def _(te):
                    te.wait_ge(sRd, Br + 16 * mch)
                    for oc in range(mch):
                        if oc >= 2:
                            te.wait_ge(sa, Ba + oc - 1)
                        for mc in range(mch):
                            last = te.matmul(pof[:, (oc % 2) * 512:(oc % 2) * 512 + 512],
                                             Owb[:, mc * 1024 + oc * 128:mc * 1024 + (oc + 1) * 128],
                                             ard[:, mc * 512:(mc + 1) * 512],
                                             start=(mc == 0), stop=(mc == mch - 1))
                        last.then_inc(stn, 1)

                @blk.scalar
                def _(sc):
                    for oc in range(mch):
                        sc.wait_ge(stn, Bt + oc + 1)
                        sc.activation(ogf[:, oc * 512:(oc + 1) * 512],
                                      pof[:, (oc % 2) * 512:(oc % 2) * 512 + 512],
                                      AF.Identity, bias=obC[:, oc:oc + 1]).then_inc(sa, 1)

                @blk.vector
                def _(ve):
                    for dc in range(8):
                        ve.wait_ge(sRd, Br + 16 * (mch + 8 + dc + 1))
                        if dc < mch:
                            ve.wait_ge(sa, Ba + dc + 1)
                        if dc >= 2:
                            ve.wait_ge(sO, Bo + 16 * (dc - 1))
                        xs = xrd[:, (dc % 2) * 512:(dc % 2) * 512 + 512]
                        fo = fout[:, (dc % 2) * 512:(dc % 2) * 512 + 512]
                        mu_ = MU[:, e5 * 512:(e5 + 1) * 512]
                        rs_ = RSTD[:, e5 * 512:(e5 + 1) * 512]
                        ve.tensor_sub(ftmp[:, :], xs, mu_)
                        ve.tensor_tensor(ftmp[:, :], ftmp[:, :], rs_, ALU.mult)
                        ve.tensor_add(fo, ftmp[:, :], xs)
                        ve.tensor_add(fo, fo, grd[:, dc * 512:(dc + 1) * 512])
                        if dc < mch:
                            ve.tensor_add(fo, fo, ogf[:, dc * 512:(dc + 1) * 512])
                        ve.engine_nop().then_inc(sv, 1)
            Bt += mch; Ba += mch; Bv += 8; Br += 16 * (mch + 16); Bo += 16 * 8
        pd.close()

    st.close()
    return nc, dbg


_NC_CACHE = {}


def _get_nc():
    key = (PHASES, tuple(DEBUG))
    if key not in _NC_CACHE:
        _NC_CACHE[key] = build_nc()
    return _NC_CACHE[key]


def _in_map_for(inputs, cid):
    b, h = cid // 2, cid % 2
    P = _perm_for(h)
    f32 = np.float32
    x = np.asarray(inputs["L0_attn_x"])[b][P]
    gamma = np.asarray(inputs["L0_attn_gamma"])
    beta = np.asarray(inputs["L0_attn_beta"])

    def colchunks(vec):
        return np.ascontiguousarray(np.asarray(vec, f32).reshape(8, 128).T)

    return {
        "xT": np.ascontiguousarray(x.T, dtype=f32),
        "posrow": np.ascontiguousarray(P[None, :OWN].astype(f32)),
        "poscol": np.ascontiguousarray(P.astype(f32).reshape(32, 128).T),
        "WqkT": np.ascontiguousarray(np.asarray(inputs["L0_attn_Wqk"], f32).T),
        "WvT": np.ascontiguousarray(np.asarray(inputs["L0_attn_Wv"], f32).T),
        "WgT": np.ascontiguousarray(np.asarray(inputs["L0_attn_Wg"], f32).T),
        "WoT": np.ascontiguousarray(np.asarray(inputs["L0_attn_Wo"], f32).T),
        "qwT": np.ascontiguousarray(np.asarray(inputs["L0_ffn_q_w"], f32).T),
        "kwT": np.ascontiguousarray(np.asarray(inputs["L0_ffn_k_w"], f32).T),
        "vwT": np.ascontiguousarray(np.asarray(inputs["L0_ffn_v_w"], f32).T),
        "owT": np.ascontiguousarray(np.asarray(inputs["L0_ffn_o_w"], f32).T),
        "pos_emb": np.ascontiguousarray(np.asarray(inputs["L0_attn_pos_emb"], f32)[None, :]),
        "gammaC": np.ascontiguousarray(gamma.reshape(4, 2, 128).transpose(2, 0, 1).reshape(128, 8).astype(f32)),
        "betaC": np.ascontiguousarray(beta.reshape(4, 2, 128).transpose(2, 0, 1).reshape(128, 8).astype(f32)),
        "g3_row": np.ascontiguousarray(gamma[3][None, :].astype(f32)),
        "b3_row": np.ascontiguousarray(beta[3][None, :].astype(f32)),
        "bqk_row": np.ascontiguousarray(np.asarray(inputs["L0_attn_bqk"], f32)[None, :]),
        "bqk_col": np.ascontiguousarray(np.asarray(inputs["L0_attn_bqk"], f32).reshape(2, 128).T),
        "bv_row": np.ascontiguousarray(np.asarray(inputs["L0_attn_bv"], f32)[None, :]),
        "bg_row": np.ascontiguousarray(np.asarray(inputs["L0_attn_bg"], f32)[None, :]),
        "vb_row": np.ascontiguousarray(np.asarray(inputs["L0_ffn_v_b"], f32)[None, :]),
        "ln1w_c": colchunks(inputs["L0_attn_ln1_w"]),
        "ln1b_c": colchunks(inputs["L0_attn_ln1_b"]),
        "lnfw_c": colchunks(inputs["L0_ffn_norm_w"]),
        "lnfb_c": colchunks(inputs["L0_ffn_norm_b"]),
        "ln2w_c": colchunks(inputs["L0_attn_ln2_w"]),
        "ln2b_c": colchunks(inputs["L0_attn_ln2_b"]),
        "qb_c": colchunks(inputs["L0_ffn_q_b"]),
        "kb_c": colchunks(inputs["L0_ffn_k_b"]),
        "bo_c": colchunks(inputs["L0_attn_bo"]),
        "ob_c": colchunks(inputs["L0_ffn_o_b"]),
    }


def kernel(**inputs):
    nc, dbg = _get_nc()
    in_maps = [_in_map_for(inputs, cid) for cid in range(8)]
    res = run_bass_kernel_spmd(nc, in_maps, core_ids=list(range(8)))
    out = np.zeros((4, 4096, 1024), dtype=np.asarray(res.results[0]["outT"]).dtype)
    for cid in range(8):
        b, h = cid // 2, cid % 2
        P = _perm_for(h)
        out[b][P[:OWN]] = res.results[cid]["outT"].T
    kernel._last_results = res.results
    return out


# version 13
# speedup vs baseline: 1.2645x; 1.2645x over previous
import numpy as np
from contextlib import ExitStack

import concourse.bass as bass
import concourse.mybir as mybir
from concourse.bass_utils import run_bass_kernel_spmd

F32 = mybir.dt.float32
BF16 = mybir.dt.bfloat16
I32 = mybir.dt.int32
AF = mybir.ActivationFunctionType
ALU = mybir.AluOpType
AXX = mybir.AxisListType.X

EPS = 1e-5
T = 4096
OWN = 2048
D = 1024
NCH = 8
MCH512 = [8, 4, 2, 1, 8, 4, 2, 1]
LN10K = float(np.log(10000.0))
PI = float(np.pi)
TWO_PI = 2.0 * np.pi
C1 = float(np.float32(TWO_PI))
_r1 = np.float64(TWO_PI) - np.float64(np.float32(TWO_PI))
C2 = float(np.float32(_r1))
C3 = float(np.float32(np.float64(_r1) - np.float64(np.float32(_r1))))
PIH1 = float(np.float32(np.pi / 2))
PIH2 = float(np.float32(np.float64(np.pi / 2) - np.float64(np.float32(np.pi / 2))))
_A = -np.log(10000.0) / 128.0
A1 = float(np.float32(_A))
A2 = float(np.float32(np.float64(_A) - np.float64(np.float32(_A))))

PHASES = 99
SUPS = 4
DEBUG = ()


def _perm_for(h):
    own = np.concatenate([np.arange(e * 1024 + h * 512, e * 1024 + h * 512 + 512) for e in range(4)])
    par = np.concatenate([np.arange(e * 1024 + (1 - h) * 512, e * 1024 + (1 - h) * 512 + 512) for e in range(4)])
    return np.concatenate([own, par])


def build_nc():
    nc = bass.Bass("TRN2", target_bir_lowering=False, detect_race_conditions=False)
    st = ExitStack()

    def dram_in(name, shape):
        return nc.dram_tensor(name, shape, F32, kind="ExternalInput").ap()

    xT = dram_in("xT", [D, T])
    posrow = dram_in("posrow", [1, OWN])
    poscol = dram_in("poscol", [128, 32])
    WqkT = dram_in("WqkT", [D, 256])
    WvT = dram_in("WvT", [D, 2048])
    WgT = dram_in("WgT", [D, 2048])
    WoT = dram_in("WoT", [2048, D])
    qwT = dram_in("qwT", [D, D])
    kwT = dram_in("kwT", [D, D])
    vwT = dram_in("vwT", [D, D])
    owT = dram_in("owT", [D, D])
    pos_emb = dram_in("pos_emb", [1, 65])
    gammaC = dram_in("gammaC", [128, 8])
    betaC = dram_in("betaC", [128, 8])
    g3_row = dram_in("g3_row", [1, 256])
    b3_row = dram_in("b3_row", [1, 256])
    bqk_row = dram_in("bqk_row", [1, 256])
    bqk_col = dram_in("bqk_col", [128, 2])
    bv_row = dram_in("bv_row", [1, 2048])
    bg_row = dram_in("bg_row", [1, 2048])
    vb_row = dram_in("vb_row", [1, D])
    ln1w_c = dram_in("ln1w_c", [128, 8]); ln1b_c = dram_in("ln1b_c", [128, 8])
    lnfw_c = dram_in("lnfw_c", [128, 8]); lnfb_c = dram_in("lnfb_c", [128, 8])
    ln2w_c = dram_in("ln2w_c", [128, 8]); ln2b_c = dram_in("ln2b_c", [128, 8])
    qb_c = dram_in("qb_c", [128, 8]); kb_c = dram_in("kb_c", [128, 8])
    bo_c = dram_in("bo_c", [128, 8]); ob_c = dram_in("ob_c", [128, 8])

    outT = nc.dram_tensor("outT", [D, OWN], F32, kind="ExternalOutput").ap()

    vhbm = nc.dram_tensor("vhbm", [T, 2048], BF16, kind="Internal").ap()
    khbm = nc.dram_tensor("khbm", [8, 128, T], BF16, kind="Internal").ap()
    qhbm = nc.dram_tensor("qhbm", [8, 128, OWN], BF16, kind="Internal").ap()
    attnhbm = nc.dram_tensor("attnhbm", [8, 128, OWN], BF16, kind="Internal").ap()
    gauhbm = nc.dram_tensor("gauhbm", [8, 128, OWN], BF16, kind="Internal").ap()

    dbg = {}

    def dbg_sb(name, src_ap, rows, cols):
        if name not in DEBUG:
            return
        dt = nc.dram_tensor("dbg_" + name, [rows, cols], F32, kind="ExternalOutput").ap()
        dbg[name] = dt
        with nc.Block("dbg_" + name) as blk:
            pdd = ExitStack()
            bounce = pdd.enter_context(nc.sbuf_tensor("dbB" + name, [rows, cols], F32, side="right")).ap()
            sdd = st.enter_context(nc.semaphore("dbS" + name))

            @blk.vector
            def _(ve):
                ve.tensor_copy(bounce[:, :], src_ap).then_inc(sdd, 1)

            @blk.sync
            def _(sy):
                sy.wait_ge(sdd, 1)
                sy.dma_start(out=dt[:, :], in_=bounce[:, :]).then_inc(sdd, 16)
                sy.wait_ge(sdd, 17)
            pdd.close()

    def dbg_hbm(name, src_dram, nchunks, rows, cols):
        if name not in DEBUG:
            return
        dt = nc.dram_tensor("dbg_" + name, [nchunks, rows, cols], F32, kind="ExternalOutput").ap()
        dbg[name] = dt
        with nc.Block("dbg_" + name) as blk:
            pdd = ExitStack()
            bb = pdd.enter_context(nc.sbuf_tensor("dhB" + name, [rows, cols], BF16, side="right")).ap()
            bf = pdd.enter_context(nc.sbuf_tensor("dhF" + name, [rows, cols], F32, side="right")).ap()
            sdd = st.enter_context(nc.semaphore("dhS" + name))
            sdo = st.enter_context(nc.semaphore("dhO" + name))

            @blk.sync
            def _(sy):
                for i in range(nchunks):
                    if i > 0:
                        sy.wait_ge(sdd, i * 17)
                    sy.dma_start(out=bb[:, :], in_=src_dram[i] if nchunks > 1 else src_dram[:, :]).then_inc(sdd, 16)

            @blk.vector
            def _(ve):
                for i in range(nchunks):
                    ve.wait_ge(sdd, i * 17 + 16)
                    if i > 0:
                        ve.wait_ge(sdo, 16 * i)
                    ve.tensor_copy(bf[:, :], bb[:, :]).then_inc(sdd, 1)

            @blk.gpsimd
            def _(gp):
                for i in range(nchunks):
                    gp.wait_ge(sdd, i * 17 + 17)
                    gp.dma_start(out=dt[i] if nchunks > 1 else dt[:, :], in_=bf[:, :]).then_inc(sdo, 16)
                gp.wait_ge(sdo, nchunks * 16)
            pdd.close()

    def sb(name, shape, dt=BF16):
        return nc.alloc_sbuf_tensor(name, shape, dt).ap()

    MU = sb("MU", [128, T])
    RSTD = sb("RSTD", [128, T])
    
    ones128 = sb("ones128", [128, 128])
    ones1 = sb("ones1", [1, 128])
    ones1f = sb("ones1f", [1, 128], F32)
    onescol = sb("onescol", [128, 1])
    ident = sb("ident", [128, 128])
    identF = sb("identF", [128, 128], F32)
    qbias = sb("qbias", [128, 64], F32)
    linkv = sb("linkv", [128, 2 * 2048])
    negB = sb("negB", [128, 32], F32)
    KMax2 = sb("KMax2", [128, 8], F32)
    QMax2 = sb("QMax2", [128, 32], F32)
    ln1w = sb("ln1w", [128, 8], F32); ln1b = sb("ln1b", [128, 8], F32)
    lnfw = sb("lnfw", [128, 8], F32); lnfb = sb("lnfb", [128, 8], F32)
    ln2w = sb("ln2w", [128, 8], F32); ln2b = sb("ln2b", [128, 8], F32)
    qbC = sb("qbC", [128, 8], F32); kbC = sb("kbC", [128, 8], F32)
    boC = sb("boC", [128, 8], F32); obC = sb("obC", [128, 8], F32)
    bqkC = sb("bqkC", [128, 2], F32)
    gamC = sb("gamC", [128, 8], F32); betC = sb("betC", [128, 8], F32)
    bqkR = sb("bqkR", [1, 256]); bvR = sb("bvR", [1, 2048])
    bqkP = sb("bqkP", [1, 256])
    ln1bB = sb("ln1bB", [128, 8]); onesRow = sb("onesRow", [1, 512])
    bgR = sb("bgR", [1, 2048]); vbR = sb("vbR", [1, D])
    g3R = sb("g3R", [1, 256], F32); b3R = sb("b3R", [1, 256], F32)
    peR = sb("peR", [1, 65], F32)
    posC = sb("posC", [128, 32], F32)
    invfR = sb("invfR", [1, 128], F32)

    trF_g = nc.sbuf_tensor("trF", [128, 2 * OWN], BF16); trF = trF_g.__enter__().ap()
    sinF = trF[:, 0:OWN]; cosF = trF[:, OWN:2 * OWN]
    z_g = nc.sbuf_tensor("z", [128, NCH * T], BF16); z = z_g.__enter__().ap()
    lk_g = nc.sbuf_tensor("lk", [128, 32 * 256], BF16); lk = lk_g.__enter__().ap()
    tmT_g = nc.sbuf_tensor("tmT", [128, 2 * 32 * 128], BF16); tmT = tmT_g.__enter__().ap()
    sinTt = tmT[:, 0:32 * 128]; cosTt = tmT[:, 32 * 128:]

    def semN(name):
        return st.enter_context(nc.semaphore(name))

    sX = semN("sX")
    sW = semN("sW")
    wctr = [0]

    def wload(gp, dst, src_dram, rearr=True):
        src = src_dram.rearrange("(c p) k -> p c k", p=128) if rearr else src_dram
        gp.dma_start(out=dst, in_=src).then_inc(sW, 16)
        wctr[0] += 16
        return wctr[0]

    p0 = ExitStack()
    posR = p0.enter_context(nc.sbuf_tensor("posR", [1, OWN], F32, side="right")).ap()
    RPf = p0.enter_context(nc.sbuf_tensor("RPf", [128, T], BF16, side="right")).ap()
    pbc = p0.enter_context(nc.psum_tensor("pbc", [128, 2 * 512], F32)).ap()
    ppe = p0.enter_context(nc.psum_tensor("ppe", [64, 65], F32)).ap()
    p0t = ExitStack()
    phF = p0t.enter_context(nc.sbuf_tensor("phF", [128, T], F32, side="right")).ap()
    tmpF = p0t.enter_context(nc.sbuf_tensor("tmpF", [128, OWN], F32, side="right")).ap()
    kiF = p0t.enter_context(nc.sbuf_tensor("kiF", [128, OWN], I32, side="right")).ap()
    dist = p0t.enter_context(nc.sbuf_tensor("dist", [64, 64], F32, side="right")).ap()
    msk = p0t.enter_context(nc.sbuf_tensor("msk", [64, 64], F32, side="right")).ap()
    peB = p0t.enter_context(nc.sbuf_tensor("peB", [64, 65], F32, side="right")).ap()
    invfB = p0t.enter_context(nc.sbuf_tensor("invfB", [128, 128], F32, side="right")).ap()
    invfW = p0t.enter_context(nc.sbuf_tensor("invfW", [128, 128], F32, side="right")).ap()

    with nc.Block("p0a") as blk:
        @blk.gpsimd
        def _(gp):
            for c in range(NCH):
                gp.dma_start(out=z[:, c * T:(c + 1) * T],
                             in_=xT[c * 128:(c + 1) * 128, :],
                             max_dma_last_dim=2048).then_inc(sX, 16)
            loads = [(posR, posrow), (posC, poscol), (peR, pos_emb),
                     (ln1w, ln1w_c), (ln1b, ln1b_c), (lnfw, lnfw_c), (lnfb, lnfb_c),
                     (ln2w, ln2w_c), (ln2b, ln2b_c), (qbC, qb_c), (kbC, kb_c),
                     (boC, bo_c), (obC, ob_c), (bqkC, bqk_col),
                     (gamC, gammaC), (betC, betaC),
                     (bqkR, bqk_row), (bvR, bv_row), (bgR, bg_row), (vbR, vb_row),
                     (g3R, g3_row), (b3R, b3_row)]
            for dst, src in loads:
                wload(gp, dst[:, :], src, rearr=False)
            gp.memset(ones128[:, :], 1.0)
            gp.memset(ones1[:, :], 1.0)
            gp.memset(ones1f[:, :], 1.0)
            gp.memset(onesRow[:, :], 1.0)
            gp.memset(onescol[:, :], 1.0)
            sid = semN("p0a_id")
            gp.memset(ident[:, :], 1.0)
            gp.memset(identF[:, :], 1.0).then_inc(sid, 1)
            gp.wait_ge(sid, 1)
            gp.affine_select(ident[:, :], ident[:, :], [[1, 128]], ALU.is_equal, 0.0,
                             channel_multiplier=-1)
            gp.affine_select(identF[:, :], identF[:, :], [[1, 128]], ALU.is_equal, 0.0,
                             channel_multiplier=-1)
            gp.iota(invfW[:, :], [[0, 128]], channel_multiplier=1,
                    allow_small_or_imprecise_dtypes=True)
            gp.iota(invfR[:, :], [[1, 128]], channel_multiplier=0,
                    allow_small_or_imprecise_dtypes=True)
            gp.wait_ge(sW, wctr[0])


    def wrap_pm_pi(ve, dstap, tmp1, tmpi):
        ve.tensor_single_scalar(tmp1, dstap, float(1.0 / TWO_PI), ALU.mult)
        ve.tensor_single_scalar(tmp1, tmp1, 0.5, ALU.add)
        ve.tensor_copy(tmpi, tmp1)
        ve.tensor_copy(tmp1, tmpi)
        ve.scalar_tensor_tensor(dstap, tmp1, -C1, dstap, ALU.mult, ALU.add)
        ve.scalar_tensor_tensor(dstap, tmp1, -C2, dstap, ALU.mult, ALU.add)
        ve.scalar_tensor_tensor(dstap, tmp1, -C3, dstap, ALU.mult, ALU.add)
        ve.tensor_single_scalar(tmp1, dstap, PI, ALU.is_gt)
        ve.scalar_tensor_tensor(dstap, tmp1, -C1, dstap, ALU.mult, ALU.add)
        ve.scalar_tensor_tensor(dstap, tmp1, -C2, dstap, ALU.mult, ALU.add)
        ve.scalar_tensor_tensor(dstap, tmp1, -C3, dstap, ALU.mult, ALU.add)
        ve.tensor_single_scalar(tmp1, dstap, -PI, ALU.is_lt)
        ve.scalar_tensor_tensor(dstap, tmp1, C1, dstap, ALU.mult, ALU.add)
        ve.scalar_tensor_tensor(dstap, tmp1, C2, dstap, ALU.mult, ALU.add)
        ve.scalar_tensor_tensor(dstap, tmp1, C3, dstap, ALU.mult, ALU.add)

    def shift_cos(ve, dstap, srcap, tmp1):
        ve.tensor_single_scalar(dstap, srcap, PIH1, ALU.add)
        ve.tensor_single_scalar(dstap, dstap, PIH2, ALU.add)
        ve.tensor_single_scalar(tmp1, dstap, PI, ALU.is_gt)
        ve.scalar_tensor_tensor(dstap, tmp1, -C1, dstap, ALU.mult, ALU.add)
        ve.scalar_tensor_tensor(dstap, tmp1, -C2, dstap, ALU.mult, ALU.add)
        ve.scalar_tensor_tensor(dstap, tmp1, -C3, dstap, ALU.mult, ALU.add)

    with nc.Block("p0b0") as blk:
        s0b = semN("p0b0_s")

        @blk.vector
        def _(ve):
            ve.tensor_single_scalar(invfB[:, :], invfW[:, :], A2, ALU.mult)
            ve.tensor_single_scalar(invfW[:, :], invfW[:, :], A1, ALU.mult)
            ve.tensor_add(invfW[:, :], invfW[:, :], invfB[:, :])
            ve.tensor_single_scalar(tmpF[0:1, 0:128], invfR[:, :], A2, ALU.mult)
            ve.tensor_single_scalar(invfR[:, :], invfR[:, :], A1, ALU.mult)
            ve.tensor_add(invfR[:, :], invfR[:, :], tmpF[0:1, 0:128])
            ve.engine_nop()
            ve.engine_nop().then_inc(s0b, 1)

    with nc.Block("p0b1") as blk:
        @blk.scalar
        def _(sc):
            sc.wait_ge(s0b, 1)
            sc.activation(invfW[:, :], invfW[:, :], AF.Exp)
            sc.activation(invfR[:, :], invfR[:, :], AF.Exp)

        @blk.gpsimd
        def _(gp):
            gp.iota(dist[:, :], [[1, 64]], channel_multiplier=-1,
                    allow_small_or_imprecise_dtypes=True)

    with nc.Block("p0b2") as blk:
        s1 = semN("p0b_t"); s2 = semN("p0b_v")

        @blk.tensor
        def _(te):
            for i in range(4):
                if i >= 2:
                    te.wait_ge(s2, i - 1)
                te.matmul(pbc[:, (i % 2) * 512:(i % 2) * 512 + 512], ones1f[:, :],
                          posR[:, i * 512:(i + 1) * 512], start=True, stop=True).then_inc(s1, 1)
            te.matmul(ppe[:, :], ones1f[:, 0:64], peR[:, :], start=True, stop=True).then_inc(s1, 1)

        @blk.vector
        def _(ve):
            for i in range(4):
                ve.wait_ge(s1, i + 1)
                ve.tensor_scalar_mul(phF[:, i * 512:(i + 1) * 512],
                                     pbc[:, (i % 2) * 512:(i % 2) * 512 + 512],
                                     invfW[:, 0:1]).then_inc(s2, 1)
            wrap_pm_pi(ve, phF[:, 0:OWN], tmpF[:, :], kiF[:, :])

    with nc.Block("p0b3") as blk:
        @blk.scalar
        def _(sc):
            sc.activation(sinF[:, :], phF[:, 0:OWN], AF.Sin)
            sc.activation(dist[:, :], dist[:, :], AF.Abs)

    with nc.Block("p0b3b") as blk:
        @blk.vector
        def _(ve):
            shift_cos(ve, phF[:, 0:OWN], phF[:, 0:OWN], tmpF[:, :])

    with nc.Block("p0b3c") as blk:
        @blk.scalar
        def _(sc):
            sc.activation(cosF[:, :], phF[:, 0:OWN], AF.Sin)

    with nc.Block("p0b4") as blk:
        s1 = semN("p0b4_t")

        @blk.tensor
        def _(te):
            te.matmul(pbc[:, 0:128], ones1f[:, :], invfR[:, :], start=True, stop=True).then_inc(s1, 1)

        @blk.vector
        def _(ve):
            ve.wait_ge(s1, 1)
            ve.tensor_copy(invfB[:, :], pbc[:, 0:128])
            for b_ in range(32):
                ve.tensor_scalar_mul(phF[:, b_ * 128:(b_ + 1) * 128], invfB[:, :],
                                     posC[:, b_:b_ + 1])
            wrap_pm_pi(ve, phF[:, 0:OWN], tmpF[:, :], kiF[:, :])
            wrap_pm_pi(ve, phF[:, OWN:T], tmpF[:, :], kiF[:, :])

    with nc.Block("p0b5") as blk:
        @blk.scalar
        def _(sc):
            sc.activation(sinTt[:, :], phF[:, :], AF.Sin)

        @blk.vector
        def _(ve):
            ve.tensor_single_scalar(dist[:, :], dist[:, :], 32.0, ALU.min)
            ve.tensor_copy(peB[:, :], ppe[:, :])
            ve.memset(qbias[0:64, :], 0.0)
            for r in range(33):
                ve.tensor_single_scalar(msk[:, :], dist[:, :], float(r), ALU.is_equal)
                ve.scalar_tensor_tensor(qbias[0:64, :], msk[:, :], peB[:, r:r + 1],
                                        qbias[0:64, :], ALU.mult, ALU.add)

    with nc.Block("p0b6") as blk:
        @blk.vector
        def _(ve):
            shift_cos(ve, phF[:, 0:OWN], phF[:, 0:OWN], tmpF[:, :])
            shift_cos(ve, phF[:, OWN:T], phF[:, OWN:T], tmpF[:, :])

        @blk.gpsimd
        def _(gp):
            sq = semN("p0b6_g")
            gp.dma_start(out=qbias[64:128, :], in_=qbias[0:64, :]).then_inc(sq, 16)
            gp.wait_ge(sq, 16)

    with nc.Block("p0b7") as blk:
        @blk.scalar
        def _(sc):
            sc.activation(cosTt[:, :], phF[:, :], AF.Sin)

    p0t.close()
    dbg_sb("sinF", sinF[:, :], 128, OWN)
    dbg_sb("cosF", cosF[:, :], 128, OWN)
    dbg_sb("sinT", sinTt[:, :], 128, T)
    dbg_sb("qbias", qbias[:, :], 128, 64)

    if PHASES >= 1:
        x2 = p0.enter_context(nc.sbuf_tensor("x2", [128, 2 * 512], BF16, side="right")).ap()
        st6 = p0.enter_context(nc.sbuf_tensor("st6", [128, 8 * 512], F32, side="right")).ap()
        psx = p0.enter_context(nc.psum_tensor("psx", [128, 2 * 512], F32)).ap()
        psx2 = p0.enter_context(nc.psum_tensor("psx2", [128, 2 * 512], F32)).ap()
        MM_ON = ST_ON = RT_ON = Z2_ON = True
        with nc.Block("p0c") as blk:
            sa = semN("p0c_a"); stn = semN("p0c_t"); sv = semN("p0c_v"); st2 = semN("p0c_t2")

            sc_order = []
            for i in range(8):
                for c in range(NCH):
                    sc_order.append(("sq", i * 8 + c))
                if i >= 1:
                    sc_order.append(("rt", i - 1))
            sc_order.append(("rt", 7))
            sq_pos = {k: sc_order.index(("sq", k)) + 1 for k in range(64)}
            rt_pos = {i: sc_order.index(("rt", i)) + 1 for i in range(8)}

            @blk.scalar
            def _(sc):
                sc.wait_ge(sX, 16 * NCH)
                for kind, k in sc_order:
                    if kind == "sq":
                        i, c = k // 8, k % 8
                        if k >= 2 and MM_ON:
                            sc.wait_ge(st2, k - 1)
                        sc.activation(x2[:, (k % 2) * 512:(k % 2) * 512 + 512],
                                      z[:, c * T + i * 512:c * T + (i + 1) * 512],
                                      AF.Square).then_inc(sa, 1)
                    elif RT_ON:
                        sc.wait_ge(sv, 2 * k + 1)
                        sc.activation(st6[:, 2 * 512:3 * 512], st6[:, 1 * 512:2 * 512], AF.Sqrt)
                        sc.activation(st6[:, 5 * 512:6 * 512], st6[:, 4 * 512:5 * 512], AF.Sqrt).then_inc(sa, 1)
                    else:
                        sc.nop().then_inc(sa, 1)

            @blk.tensor
            def _(te):
                if not MM_ON:
                    return
                k = 0
                for i in range(8):
                    pp = i % 2
                    if i >= 2 and ST_ON:
                        te.wait_ge(sv, 2 * (i - 2) + 1)
                    for c in range(NCH):
                        te.wait_ge(sa, sq_pos[k])
                        te.matmul(psx[:, pp * 512:(pp + 1) * 512], ones128[:, :],
                                  z[:, c * T + i * 512:c * T + (i + 1) * 512],
                                  start=(c == 0), stop=(c == NCH - 1))
                        te.matmul(psx2[:, pp * 512:(pp + 1) * 512], ones128[:, :],
                                  x2[:, (k % 2) * 512:(k % 2) * 512 + 512],
                                  start=(c == 0), stop=(c == NCH - 1)).then_inc(st2, 1)
                        k += 1
                    te.wait_ge(st2, 8 * (i + 1))
                    te.nop().then_inc(stn, 1)

            @blk.vector
            def _(ve):
                if not ST_ON:
                    return
                sl = [st6[:, j * 512:(j + 1) * 512] for j in range(8)]
                for i in range(8):
                    ve.wait_ge(stn, i + 1)
                    pp = i % 2
                    px = psx[:, pp * 512:(pp + 1) * 512]
                    px2 = psx2[:, pp * 512:(pp + 1) * 512]
                    ve.tensor_single_scalar(sl[0], px, 1.0 / D, ALU.mult)
                    ve.tensor_single_scalar(sl[1], px2, 1.0 / D, ALU.mult)
                    ve.tensor_tensor(sl[3], sl[0], sl[0], ALU.mult)
                    ve.tensor_sub(sl[1], sl[1], sl[3])
                    ve.tensor_single_scalar(sl[4], sl[1], EPS, ALU.mult)
                    ve.tensor_add(sl[4], sl[4], sl[1])
                    ve.tensor_single_scalar(sl[4], sl[4], EPS * EPS, ALU.add)
                    ve.tensor_single_scalar(sl[1], sl[1], EPS, ALU.add)
                    ve.tensor_copy(MU[:, i * 512:(i + 1) * 512], sl[0])
                    ve.engine_nop()
                    ve.engine_nop().then_inc(sv, 1)
                    if not RT_ON:
                        ve.engine_nop().then_inc(sv, 1)
                        continue
                    ve.wait_ge(sa, rt_pos[i])
                    ve.reciprocal(sl[3], sl[2])
                    ve.tensor_tensor(sl[6], sl[3], sl[3], ALU.mult)
                    ve.tensor_tensor(sl[6], sl[6], sl[1], ALU.mult)
                    ve.tensor_scalar(sl[6], sl[6], -0.5, 1.5, ALU.mult, ALU.add)
                    ve.tensor_tensor(sl[3], sl[3], sl[6], ALU.mult)
                    ve.tensor_copy(RSTD[:, i * 512:(i + 1) * 512], sl[3])
                    ve.reciprocal(sl[3], sl[5])
                    ve.tensor_tensor(sl[6], sl[3], sl[3], ALU.mult)
                    ve.tensor_tensor(sl[6], sl[6], sl[4], ALU.mult)
                    ve.tensor_scalar(sl[6], sl[6], -0.5, 1.5, ALU.mult, ALU.add)
                    ve.tensor_tensor(sl[3], sl[3], sl[6], ALU.mult)
                    ve.tensor_copy(RPf[:, i * 512:(i + 1) * 512], sl[3])
                    ve.engine_nop().then_inc(sv, 1)

        with nc.Block("p0c2") as blk:
            @blk.vector
            def _(ve):
                if not Z2_ON:
                    return
                for c in range(NCH):
                    ve.tensor_sub(z[:, c * T:(c + 1) * T], z[:, c * T:(c + 1) * T], MU[:, :])
                    ve.tensor_mul(z[:, c * T:(c + 1) * T], z[:, c * T:(c + 1) * T], RPf[:, :])

        p0.close()
        dbg_sb("MU", MU[0:1, :], 1, T)
        dbg_sb("RSTD", RSTD[0:1, :], 1, T)
        for c in range(NCH):
            dbg_sb("z%d" % c, z[:, c * T:(c + 1) * T], 128, T)

    if PHASES >= 2:
        pa1 = ExitStack()
        Wqkb = pa1.enter_context(nc.sbuf_tensor("Wqkb", [128, 8 * 256], BF16, side="right")).ap()
        G3b = pa1.enter_context(nc.sbuf_tensor("G3b", [128, 256], BF16, side="right")).ap()
        B3b = pa1.enter_context(nc.sbuf_tensor("B3b", [128, 256], BF16, side="right")).ap()
        qksb = pa1.enter_context(nc.sbuf_tensor("qksb", [128, 2 * 256], BF16, side="right")).ap()
        lkp = pa1.enter_context(nc.sbuf_tensor("lkp", [128, 256], F32, side="right")).ap()
        pqk = pa1.enter_context(nc.psum_tensor("pqk", [128, 2 * 512], F32)).ap()

        with nc.Block("pa1_load") as blk:
            @blk.gpsimd
            def _(gp):
                w = wload(gp, Wqkb[:, :], WqkT)
                gp.wait_ge(sW, w)

        with nc.Block("pa1_pre") as blk:
            s1 = semN("pa1p_t"); s2 = semN("pa1p_v")

            @blk.vector
            def _(ve):
                ve.tensor_copy(ln1bB[:, :], ln1b[:, :]).then_inc(s2, 1)

            @blk.tensor
            def _(te):
                te.matmul(pqk[:, 0:256], ones1f[:, :], g3R[:, :], start=True, stop=True)
                te.matmul(pqk[:, 512:768], ones1f[:, :], b3R[:, :], start=True, stop=True).then_inc(s1, 1)
                te.wait_ge(s2, 1)
                for c in range(NCH):
                    last = te.matmul(pqk[0:1, 768:1024], ln1bB[:, c:c + 1], Wqkb[:, c * 256:(c + 1) * 256],
                                     start=(c == 0), stop=(c == NCH - 1))
                last.then_inc(s1, 1)

            @blk.vector
            def _(ve):
                ve.wait_ge(s1, 2)
                ve.tensor_copy(G3b[:, :], pqk[:, 0:256])
                ve.tensor_copy(B3b[:, :], pqk[:, 512:768])
                ve.tensor_add(bqkP[:, :], pqk[0:1, 768:1024], bqkR[:, :])
                for c in range(NCH):
                    ve.tensor_scalar_mul(Wqkb[:, c * 256:(c + 1) * 256],
                                         Wqkb[:, c * 256:(c + 1) * 256], ln1w[:, c:c + 1])
                ve.engine_nop().then_inc(s2, 1)

        with nc.Block("pa1") as blk:
            stn = semN("pa1_t"); sa = semN("pa1_a"); sv = semN("pa1_v")

            @blk.tensor
            def _(te):
                for b_ in range(32):
                    pp = b_ % 2
                    if b_ >= 2:
                        te.wait_ge(sa, b_ - 1)
                    te.matmul(pqk[:, pp * 512:pp * 512 + 256], ones1[:, :], bqkP[:, :],
                              start=True, stop=False)
                    for c in range(NCH):
                        last = te.matmul(pqk[:, pp * 512:pp * 512 + 256],
                                         z[:, c * T + b_ * 128:c * T + b_ * 128 + 128],
                                         Wqkb[:, c * 256:(c + 1) * 256],
                                         start=False, stop=(c == NCH - 1))
                    last.then_inc(stn, 1)

            @blk.scalar
            def _(sc):
                for b_ in range(32):
                    pp = b_ % 2
                    sc.wait_ge(stn, b_ + 1)
                    if b_ >= 2:
                        sc.wait_ge(sv, b_ - 1)
                    sc.activation(qksb[:, pp * 256:(pp + 1) * 256],
                                  pqk[:, pp * 512:pp * 512 + 256], AF.Silu).then_inc(sa, 1)

            @blk.vector
            def _(ve):
                for b_ in range(32):
                    pp = b_ % 2
                    ve.wait_ge(sa, b_ + 1)
                    q_ = qksb[:, pp * 256:(pp + 1) * 256]
                    ve.tensor_mul(lkp[:, :], q_, G3b[:, :])
                    ve.tensor_add(lkp[:, :], lkp[:, :], B3b[:, :])
                    x1 = lkp[:, 0:128]; x2_ = lkp[:, 128:256]
                    co = cosTt[:, b_ * 128:(b_ + 1) * 128]
                    si = sinTt[:, b_ * 128:(b_ + 1) * 128]
                    o1 = lk[:, b_ * 256:b_ * 256 + 128]
                    o2 = lk[:, b_ * 256 + 128:b_ * 256 + 256]
                    ve.tensor_mul(o1, x1, co)
                    ve.tensor_mul(o2, x2_, si)
                    ve.tensor_sub(o1, o1, o2)
                    ve.tensor_mul(o2, x2_, co)
                    ve.tensor_mul(x1, x1, si)
                    ve.tensor_add(o2, o2, x1)
                    ve.engine_nop().then_inc(sv, 1)
        pa1.close()
        tmT_g.__exit__(None, None, None)
        dbg_sb("lk", lk[:, :], 128, 32 * 256)

    if PHASES >= 3:
        pa2 = ExitStack()
        Wvb = pa2.enter_context(nc.sbuf_tensor("Wvb", [128, 8 * 2048], BF16, side="right")).ap()
        vasm = pa2.enter_context(nc.sbuf_tensor("vasm", [128, 2 * 2048], BF16, side="right")).ap()
        bvP = pa2.enter_context(nc.sbuf_tensor("bvP", [1, 2048], BF16, side="right")).ap()
        pv = pa2.enter_context(nc.psum_tensor("pv", [128, 2 * 1024], F32)).ap()
        plk = pa2.enter_context(nc.psum_tensor("plk", [128, 4 * 512], F32)).ap()

        with nc.Block("pa2_load") as blk:
            @blk.gpsimd
            def _(gp):
                w = wload(gp, Wvb[:, :], WvT)
                gp.wait_ge(sW, w)

        with nc.Block("pa2_fold") as blk:
            s1 = semN("pa2f_t")

            @blk.tensor
            def _(te):
                for c in range(NCH):
                    for j in range(4):
                        last = te.matmul(pv[0:1, j * 512:(j + 1) * 512], ln1bB[:, c:c + 1],
                                         Wvb[:, c * 2048 + j * 512:c * 2048 + (j + 1) * 512],
                                         start=(c == 0), stop=(c == NCH - 1))
                last.then_inc(s1, 1)

            @blk.vector
            def _(ve):
                ve.wait_ge(s1, 1)
                ve.tensor_add(bvP[:, :], pv[0:1, 0:2048], bvR[:, :])
                for c in range(NCH):
                    ve.tensor_scalar_mul(Wvb[:, c * 2048:(c + 1) * 2048],
                                         Wvb[:, c * 2048:(c + 1) * 2048], ln1w[:, c:c + 1])

        with nc.Block("pa2") as blk:
            stn = semN("pa2_t"); sa = semN("pa2_a"); sSp = semN("pa2_sp")

            @blk.tensor
            def _(te):
                k = 0
                for b_ in range(32):
                    for eh in range(2):
                        pp = k % 2
                        if k >= 2:
                            te.wait_ge(sa, k - 1)
                        for j in range(2):
                            te.matmul(pv[:, pp * 1024 + j * 512:pp * 1024 + (j + 1) * 512],
                                      ones1[:, :],
                                      bvP[:, eh * 1024 + j * 512:eh * 1024 + (j + 1) * 512],
                                      start=True, stop=False)
                        for c in range(NCH):
                            for j in range(2):
                                last = te.matmul(pv[:, pp * 1024 + j * 512:pp * 1024 + (j + 1) * 512],
                                                 z[:, c * T + b_ * 128:c * T + b_ * 128 + 128],
                                                 Wvb[:, c * 2048 + eh * 1024 + j * 512:
                                                     c * 2048 + eh * 1024 + (j + 1) * 512],
                                                 start=False, stop=(c == NCH - 1))
                        last.then_inc(stn, 1)
                        k += 1
                    te.wait_ge(sa, 2 * b_ + 1)
                    for dk in range(2):
                        for j in range(2):
                            te.matmul(plk[:, (dk * 2 + j) * 512:(dk * 2 + j + 1) * 512],
                                      lk[:, b_ * 256 + dk * 128:b_ * 256 + (dk + 1) * 128],
                                      vasm[:, (b_ % 2) * 2048 + j * 512:
                                           (b_ % 2) * 2048 + (j + 1) * 512],
                                      start=(b_ == 0), stop=(b_ == 31))
                te.wait_ge(stn, 64)
                te.nop().then_inc(stn, 1)

            @blk.scalar
            def _(sc):
                k = 0
                for b_ in range(32):
                    for eh in range(2):
                        pp = k % 2
                        sc.wait_ge(stn, b_ * 2 + eh + 1)
                        if b_ >= 2 and eh == 0:
                            sc.wait_ge(sSp, 16 * (b_ - 1))
                        sc.activation(vasm[:, (b_ % 2) * 2048 + eh * 1024:
                                           (b_ % 2) * 2048 + (eh + 1) * 1024],
                                      pv[:, pp * 1024:(pp + 1) * 1024], AF.Silu).then_inc(sa, 1)
                        k += 1

            @blk.sync
            def _(sy):
                for b_ in range(32):
                    sy.wait_ge(sa, b_ * 2 + 2)
                    sy.dma_start(out=vhbm[b_ * 128:(b_ + 1) * 128, :],
                                 in_=vasm[:, (b_ % 2) * 2048:(b_ % 2) * 2048 + 2048]).then_inc(sSp, 16)
                sy.wait_ge(sSp, 16 * 32)

            @blk.vector
            def _(ve):
                ve.wait_ge(stn, 65)
                for dk in range(2):
                    for j in range(2):
                        ve.tensor_single_scalar(linkv[:, dk * 2048 + j * 512:dk * 2048 + (j + 1) * 512],
                                                plk[:, (dk * 2 + j) * 512:(dk * 2 + j + 1) * 512],
                                                1.0 / T, ALU.mult)

        with nc.Block("pa2b") as blk:
            stn = semN("pa2b_t"); sRd = semN("pa2b_rd")
            vrd = vasm

            @blk.sync
            def _(sy):
                for b_ in range(32):
                    if b_ >= 2:
                        sy.wait_ge(stn, b_ - 1)
                    sy.dma_start(out=vrd[:, (b_ % 2) * 2048:(b_ % 2) * 2048 + 1024],
                                 in_=vhbm[b_ * 128:(b_ + 1) * 128, 1024:2048]).then_inc(sRd, 16)

            @blk.tensor
            def _(te):
                for b_ in range(32):
                    te.wait_ge(sRd, 16 * (b_ + 1))
                    for dk in range(2):
                        for j in range(2):
                            last = te.matmul(plk[:, (dk * 2 + j) * 512:(dk * 2 + j + 1) * 512],
                                             lk[:, b_ * 256 + dk * 128:b_ * 256 + (dk + 1) * 128],
                                             vrd[:, (b_ % 2) * 2048 + j * 512:(b_ % 2) * 2048 + (j + 1) * 512],
                                             start=(b_ == 0), stop=(b_ == 31))
                    last.then_inc(stn, 1)

            @blk.vector
            def _(ve):
                ve.wait_ge(stn, 32)
                for dk in range(2):
                    for j in range(2):
                        ve.tensor_single_scalar(linkv[:, dk * 2048 + 1024 + j * 512:
                                                      dk * 2048 + 1024 + (j + 1) * 512],
                                                plk[:, (dk * 2 + j) * 512:(dk * 2 + j + 1) * 512],
                                                1.0 / T, ALU.mult)
        pa2.close()
        lk_g.__exit__(None, None, None)
        dbg_sb("linkv0", linkv[:, 0:2048], 128, 2048)
        dbg_sb("linkv1", linkv[:, 2048:4096], 128, 2048)
        dbg_hbm("vspill", vhbm.rearrange("(n p) k -> n p k", p=128), 32, 128, 2048)

    Vtm_g = None
    if PHASES >= 4:
        Vtm_g = nc.sbuf_tensor("Vtm", [128, 32 * 1024], BF16)
        Vtm = Vtm_g.__enter__().ap()
        pa3 = ExitStack()
        Wfb = pa3.enter_context(nc.sbuf_tensor("Wfb", [128, 8 * 1024], BF16, side="right")).ap()
        xfb = pa3.enter_context(nc.sbuf_tensor("xfb", [128, 8 * 512], BF16, side="right")).ap()
        kst = pa3.enter_context(nc.sbuf_tensor("kst", [128, 2 * 512], BF16, side="right")).ap()
        k2t = pa3.enter_context(nc.sbuf_tensor("k2t", [128, 2 * 512], BF16, side="right")).ap()
        nrm = pa3.enter_context(nc.sbuf_tensor("nrm", [128, 2], F32, side="right")).ap()
        pk = pa3.enter_context(nc.psum_tensor("pk", [128, 2 * 512], F32)).ap()
        pn = pa3.enter_context(nc.psum_tensor("pn", [128, 2 * 512], F32)).ap()
        pvf = pa3.enter_context(nc.psum_tensor("pvf", [128, 2 * 1024], F32)).ap()

        def kq_sweep(tag, wdram, bcol, hbm_out, qmax_store):
            nblocks = 4 if qmax_store else 8
            with nc.Block(tag + "_load") as blk:
                @blk.gpsimd
                def _(gp):
                    w = wload(gp, Wfb[:, :], wdram)
                    gp.wait_ge(sW, w)

            with nc.Block(tag) as blk:
                stn = semN(tag + "_t"); sa = semN(tag + "_a"); sv = semN(tag + "_v")
                sSp = semN(tag + "_sp")
                vdone = lambda kk: (kk // 8 + 1) + kk + 1
                kq_seq = []
                for kk in range(nblocks * 8):
                    kq_seq.append(("K", kk))
                    if kk % 8 >= 1:
                        kq_seq.append(("N", kk - 1))
                    if kk % 8 == 7:
                        kq_seq.append(("N", kk))
                kpos = {}
                npos = {}
                for i, (t_, k_) in enumerate(kq_seq):
                    (kpos if t_ == "K" else npos)[k_] = i + 1

                @blk.vector
                def _(ve):
                    for t5 in range(nblocks):
                        mch = MCH512[t5]
                        if t5 > 0:
                            ve.wait_ge(stn, kpos[t5 * 8 - 1])
                        for c in range(mch):
                            ve.tensor_scalar(xfb[:, c * 512:(c + 1) * 512],
                                             z[:, c * T + t5 * 512:c * T + (t5 + 1) * 512],
                                             lnfw[:, c:c + 1], lnfb[:, c:c + 1],
                                             ALU.mult, ALU.add)
                        ve.engine_nop().then_inc(sv, 1)
                        for dk in range(8):
                            kk = t5 * 8 + dk
                            pp = kk % 2
                            ve.wait_ge(stn, npos[kk])
                            ve.reduce_max(nrm[:, pp:pp + 1],
                                          pn[:, pp * 512:(pp + 1) * 512], AXX)
                            if qmax_store:
                                ve.tensor_copy(QMax2[:, dk * 4 + t5:dk * 4 + t5 + 1],
                                               nrm[:, pp:pp + 1])
                            else:
                                if t5 == 0:
                                    ve.tensor_copy(KMax2[:, dk:dk + 1], nrm[:, pp:pp + 1])
                                else:
                                    ve.tensor_max(KMax2[:, dk:dk + 1], KMax2[:, dk:dk + 1],
                                                  nrm[:, pp:pp + 1])
                            ve.engine_nop().then_inc(sv, 1)

                @blk.tensor
                def _(te):
                    for t_, kk in kq_seq:
                        t5 = kk // 8
                        mch = MCH512[t5]
                        pp = kk % 2
                        if t_ == "K":
                            if kk % 8 == 0:
                                te.wait_ge(sv, t5 * 9 + 1)
                            if kk >= 2:
                                te.wait_ge(sa, kk - 1)
                            for c in range(mch):
                                last = te.matmul(pk[:, pp * 512:(pp + 1) * 512],
                                                 Wfb[:, c * 1024 + (kk % 8) * 128:
                                                     c * 1024 + (kk % 8 + 1) * 128],
                                                 xfb[:, c * 512:(c + 1) * 512],
                                                 start=(c == 0), stop=(c == mch - 1))
                            last.then_inc(stn, 1)
                        else:
                            te.wait_ge(sa, kk + 1)
                            if kk >= 2:
                                te.wait_ge(sv, vdone(kk - 2))
                            te.matmul(pn[:, pp * 512:(pp + 1) * 512], ones128[:, :],
                                      k2t[:, pp * 512:(pp + 1) * 512],
                                      start=True, stop=True).then_inc(stn, 1)

                @blk.scalar
                def _(sc):
                    for t5 in range(nblocks):
                        for dk in range(8):
                            kk = t5 * 8 + dk
                            pp = kk % 2
                            sc.wait_ge(stn, kpos[kk])
                            if kk >= 2:
                                sc.wait_ge(sSp, 16 * (kk - 1))
                            sc.activation(kst[:, pp * 512:(pp + 1) * 512],
                                          pk[:, pp * 512:(pp + 1) * 512], AF.Identity,
                                          bias=bcol[:, dk:dk + 1])
                            sc.activation(k2t[:, pp * 512:(pp + 1) * 512],
                                          kst[:, pp * 512:(pp + 1) * 512],
                                          AF.Square).then_inc(sa, 1)

                @blk.sync
                def _(sy):
                    for t5 in range(nblocks):
                        for dk in range(8):
                            kk = t5 * 8 + dk
                            pp = kk % 2
                            sy.wait_ge(sa, kk + 1)
                            sy.dma_start(out=hbm_out[dk, :, t5 * 512:(t5 + 1) * 512],
                                         in_=kst[:, pp * 512:(pp + 1) * 512]).then_inc(sSp, 16)
                    sy.wait_ge(sSp, 16 * nblocks * 8)

        kq_sweep("pa3k", kwT, kbC, khbm, False)
        kq_sweep("pa3q", qwT, qbC, qhbm, True)

        with nc.Block("pa3nb") as blk:
            @blk.vector
            def _(ve):
                for hh in range(8):
                    for qb in range(4):
                        ve.tensor_mul(negB[:, hh * 4 + qb:hh * 4 + qb + 1],
                                      QMax2[:, hh * 4 + qb:hh * 4 + qb + 1],
                                      KMax2[:, hh:hh + 1])

        with nc.Block("pa3nb2") as blk:
            @blk.scalar
            def _(sc):
                sc.activation(negB[:, :], negB[:, :], AF.Sqrt)
                sc.mul(negB[:, :], negB[:, :], -1.0 / 32.0)

        with nc.Block("pa3v_load") as blk:
            @blk.gpsimd
            def _(gp):
                w = wload(gp, Wfb[:, :], vwT)
                gp.wait_ge(sW, w)

        with nc.Block("pa3v") as blk:
            stn = semN("pa3v_t"); sv = semN("pa3v_v")
            vcopy = lambda b: (b // 4 + 1) + b + 1

            @blk.vector
            def _(ve):
                for t5 in range(8):
                    mch = MCH512[t5]
                    if t5 > 0:
                        ve.wait_ge(stn, 4 * t5)
                    for c in range(mch):
                        ve.tensor_scalar(xfb[:, c * 512:(c + 1) * 512],
                                         z[:, c * T + t5 * 512:c * T + (t5 + 1) * 512],
                                         lnfw[:, c:c + 1], lnfb[:, c:c + 1],
                                         ALU.mult, ALU.add)
                    ve.engine_nop().then_inc(sv, 1)
                    for tb in range(4):
                        b_ = t5 * 4 + tb
                        ve.wait_ge(stn, b_ + 1)
                        ve.tensor_copy(Vtm[:, b_ * 1024:(b_ + 1) * 1024],
                                       pvf[:, (b_ % 2) * 1024:(b_ % 2) * 1024 + 1024])
                        ve.engine_nop().then_inc(sv, 1)

            @blk.tensor
            def _(te):
                for t5 in range(8):
                    mch = MCH512[t5]
                    te.wait_ge(sv, t5 * 5 + 1)
                    for tb in range(4):
                        b_ = t5 * 4 + tb
                        pp = b_ % 2
                        if b_ >= 2:
                            te.wait_ge(sv, vcopy(b_ - 2))
                        for j in range(2):
                            te.matmul(pvf[:, pp * 1024 + j * 512:pp * 1024 + (j + 1) * 512],
                                      ones1[:, :], vbR[:, j * 512:(j + 1) * 512],
                                      start=True, stop=False)
                        for c in range(mch):
                            for j in range(2):
                                last = te.matmul(pvf[:, pp * 1024 + j * 512:pp * 1024 + (j + 1) * 512],
                                                 xfb[:, c * 512 + tb * 128:c * 512 + tb * 128 + 128],
                                                 Wfb[:, c * 1024 + j * 512:c * 1024 + (j + 1) * 512],
                                                 start=False, stop=(c == mch - 1))
                        last.then_inc(stn, 1)
        pa3.close()
        dbg_hbm("Kf", khbm, 8, 128, T)
        dbg_hbm("Qf", qhbm, 8, 128, OWN)
        dbg_sb("negB", negB[:, :], 128, 32)
        for b_ in (0, 17):
            dbg_sb("Vtm%d" % b_, Vtm[:, b_ * 1024:(b_ + 1) * 1024], 128, 1024)

    if PHASES >= 5:
        pbs = ExitStack()
        Khb = pbs.enter_context(nc.sbuf_tensor("Khb", [128, 2 * T], BF16, side="right")).ap()
        Qrd = pbs.enter_context(nc.sbuf_tensor("Qrd", [128, 2 * 512], BF16, side="right")).ap()
        PT = pbs.enter_context(nc.sbuf_tensor("PT", [128, 2 * 1024], BF16, side="right")).ap()
        sums = pbs.enter_context(nc.sbuf_tensor("sums", [1, 512], F32, side="right")).ap()
        rsum = pbs.enter_context(nc.sbuf_tensor("rsum", [128, 512], F32, side="right")).ap()
        atile = pbs.enter_context(nc.sbuf_tensor("atile", [128, 2 * 512], BF16, side="right")).ap()
        pS = pbs.enter_context(nc.psum_tensor("pS", [128, 2 * 1024], F32)).ap()
        pPV = pbs.enter_context(nc.psum_tensor("pPV", [128, 2 * 512], F32)).ap()
        pSmB = pbs.enter_context(nc.psum_tensor("pSmB", [128, 2 * 512], F32)).ap()

        GROUPS = [2] * 16
        NG = len(GROUPS)

        issue = []
        issue.append(("K", 0))
        for hh in range(8):
            for qb in range(4):
                it = hh * 4 + qb
                issue.append(("Q", it))
                if qb == 0 and hh < 7:
                    issue.append(("K", hh + 1))
        posK = {hh: issue.index(("K", hh)) + 1 for hh in range(8)}
        posQ = {it: issue.index(("Q", it)) + 1 for it in range(32)}

        with nc.Block("pb") as blk:
            stn = semN("pb_t"); sa = semN("pb_a"); sv = semN("pb_v")
            sB = semN("pb_b")
            sRd = semN("pb_rd"); sSp = semN("pb_sp")

            @blk.sync
            def _(sy):
                for ent in issue:
                    if ent[0] == "K":
                        hh = ent[1]
                        sy.dma_start(out=Khb[:, (hh % 2) * T:(hh % 2) * T + T],
                                     in_=khbm[hh, :, :]).then_inc(sRd, 16)
                    else:
                        it = ent[1]
                        if it >= 2:
                            sy.wait_ge(sv, 2 * it - 2)
                        sy.dma_start(out=Qrd[:, (it % 2) * 512:(it % 2) * 512 + 512],
                                     in_=qhbm[it // 4, :, (it % 4) * 512:(it % 4) * 512 + 512]
                                     ).then_inc(sRd, 16)
                        if it >= 1:
                            sy.wait_ge(sv, 2 * it)
                            sy.dma_start(out=attnhbm[(it - 1) // 4, :,
                                                     ((it - 1) % 4) * 512:((it - 1) % 4) * 512 + 512],
                                         in_=atile[:, ((it - 1) % 2) * 512:((it - 1) % 2) * 512 + 512]
                                         ).then_inc(sSp, 16)
                sy.wait_ge(sv, 64)
                sy.dma_start(out=attnhbm[7, :, 3 * 512:4 * 512],
                             in_=atile[:, 1 * 512:2 * 512]).then_inc(sSp, 16)
                sy.wait_ge(sSp, 16 * 32)

            @blk.tensor
            def _(te):
                pending = [None]
                ktof = [0]
                for sz in GROUPS:
                    ktof.append(ktof[-1] + sz)
                for hh in range(8):
                    for qb in range(4):
                        it = hh * 4 + qb
                        abase = it * NG
                        te.wait_ge(sRd, 16 * max(posK[hh], posQ[it]))

                        def emit_S(g):
                            pp = g % 2
                            te.wait_ge(sa, abase + max(g - 1, 0))
                            sz = GROUPS[g]
                            last = None
                            for j in range(sz):
                                last = te.matmul(pS[:, pp * 1024 + j * 512:pp * 1024 + (j + 1) * 512],
                                                 Khb[:, (hh % 2) * T + (ktof[g] + j) * 128:
                                                     (hh % 2) * T + (ktof[g] + j + 1) * 128],
                                                 Qrd[:, (it % 2) * 512:(it % 2) * 512 + 512],
                                                 start=True, stop=True)
                            last.then_inc(stn, 1)

                        def emit_PV(g):
                            pp = g % 2
                            te.wait_ge(sa, abase + g + 1)
                            if g == 0 and it >= 2:
                                te.wait_ge(sv, 2 * it - 2)
                            sz = GROUPS[g]
                            last = None
                            for j in range(sz):
                                ktj = ktof[g] + j
                                te.matmul(pPV[:, (it % 2) * 512:(it % 2) * 512 + 512],
                                          Vtm[:, ktj * 1024 + hh * 128:ktj * 1024 + (hh + 1) * 128],
                                          PT[:, pp * 1024 + j * 512:pp * 1024 + (j + 1) * 512],
                                          start=(ktj == 0), stop=(ktj == 31))
                                last = te.matmul(pSmB[0:1, (it % 2) * 512:(it % 2) * 512 + 512],
                                                 onescol[:, :],
                                                 PT[:, pp * 1024 + j * 512:pp * 1024 + (j + 1) * 512],
                                                 start=(ktj == 0), stop=(ktj == 31))
                            last.then_inc(stn, 1)

                        emit_S(0)
                        if pending[0] is not None:
                            pending[0]()
                        for g in range(1, NG):
                            emit_S(g)
                            emit_PV(g - 1)
                        emit_PV(NG - 1)

                        def bc(it=it):
                            te.wait_ge(sv, 2 * it + 1)
                            te.matmul(pSmB[:, (it % 2) * 512:(it % 2) * 512 + 512],
                                      ones1f[:, :], sums[:, :],
                                      start=True, stop=True).then_inc(sB, 1)
                        pending[0] = bc
                pending[0]()

            @blk.scalar
            def _(sc):
                for it in range(32):
                    for g, sz in enumerate(GROUPS):
                        pp = g % 2
                        sc.wait_ge(stn, 32 * it + (1 if g == 0 else 2 * g))
                        sc.activation(PT[:, pp * 1024:pp * 1024 + sz * 512],
                                      pS[:, pp * 1024:pp * 1024 + sz * 512],
                                      AF.Exp, bias=negB[:, it:it + 1],
                                      scale=1.0 / 32.0).then_inc(sa, 1)

            @blk.vector
            def _(ve):
                for it in range(32):
                    ve.wait_ge(stn, 32 * it + 32)
                    ve.tensor_copy(sums[:, :],
                                   pSmB[0:1, (it % 2) * 512:(it % 2) * 512 + 512])
                    ve.tensor_copy(rsum[0:1, 0:128], sums[:, 0:128]).then_inc(sv, 1)
                    ve.wait_ge(sB, it + 1)
                    if it >= 2:
                        ve.wait_ge(sSp, 16 * (it - 1))
                    ve.reciprocal(rsum[:, :], pSmB[:, (it % 2) * 512:(it % 2) * 512 + 512])
                    ve.tensor_mul(atile[:, (it % 2) * 512:(it % 2) * 512 + 512],
                                  pPV[:, (it % 2) * 512:(it % 2) * 512 + 512],
                                  rsum[:, :])
                    ve.tensor_copy(sums[:, 0:128],
                                   atile[0:1, (it % 2) * 512:(it % 2) * 512 + 128]
                                   ).then_inc(sv, 1)
        pbs.close()
        Vtm_g.__exit__(None, None, None)
        dbg_hbm("attn", attnhbm, 8, 128, OWN)

    if PHASES >= 6:
        gatehbm = nc.dram_tensor("gatehbm", [OWN, 2048], BF16, kind="Internal").ap()
        pc = ExitStack()
        zo = pc.enter_context(nc.sbuf_tensor("zo", [128, 8 * OWN], BF16, side="right")).ap()
        lq = pc.enter_context(nc.sbuf_tensor("lq", [128, 2 * OWN], BF16, side="right")).ap()
        P2 = pc.enter_context(nc.sbuf_tensor("P2", [128, 16 * 128], BF16, side="right")).ap()
        ptmp = pc.enter_context(nc.sbuf_tensor("ptmp", [128, 128], F32, side="right")).ap()

        with nc.Block("pc_z") as blk:
            @blk.vector
            def _(ve):
                for c in range(NCH):
                    ve.tensor_copy(zo[:, c * OWN:(c + 1) * OWN], z[:, c * T:c * T + OWN])
        z_g.__exit__(None, None, None)

        pcH = ExitStack()
        qkF = pcH.enter_context(nc.sbuf_tensor("qkF", [128, 2 * OWN], BF16, side="right")).ap()
        qq = pcH.enter_context(nc.sbuf_tensor("qq", [128, 2 * OWN], BF16, side="right")).ap()
        qkk = pcH.enter_context(nc.sbuf_tensor("qkk", [128, 2 * OWN], BF16, side="right")).ap()
        hpre = pcH.enter_context(nc.sbuf_tensor("hpre", [128, 2 * OWN], BF16, side="right")).ap()

        with nc.Block("pc_qk") as blk:
            pq = ExitStack()
            Wqkb2 = pq.enter_context(nc.sbuf_tensor("Wqkb2", [128, 8 * 256], BF16, side="right")).ap()
            pqk2 = pq.enter_context(nc.psum_tensor("pqk2", [128, 2 * 512], F32)).ap()
            stn = semN("pcqk_t"); sa = semN("pcqk_a")

            sfold = semN("pcqk_f")

            @blk.gpsimd
            def _(gp):
                wload(gp, Wqkb2[:, :], WqkT)

            @blk.vector
            def _(ve):
                ve.wait_ge(sW, wctr[0])
                for c in range(NCH):
                    ve.tensor_scalar_mul(Wqkb2[:, c * 256:(c + 1) * 256],
                                         Wqkb2[:, c * 256:(c + 1) * 256], ln1w[:, c:c + 1])
                ve.engine_nop().then_inc(sfold, 1)

            @blk.tensor
            def _(te):
                te.wait_ge(sfold, 1)
                k = 0
                for dk in range(2):
                    for t5 in range(4):
                        pp = k % 2
                        if k >= 2:
                            te.wait_ge(sa, k - 1)
                        te.matmul(pqk2[:, pp * 512:(pp + 1) * 512],
                                  bqkP[0:1, dk * 128:(dk + 1) * 128],
                                  onesRow[:, :], start=True, stop=False)
                        for c in range(NCH):
                            last = te.matmul(pqk2[:, pp * 512:(pp + 1) * 512],
                                             Wqkb2[:, c * 256 + dk * 128:c * 256 + (dk + 1) * 128],
                                             zo[:, c * OWN + t5 * 512:c * OWN + (t5 + 1) * 512],
                                             start=False, stop=(c == NCH - 1))
                        last.then_inc(stn, 1)
                        k += 1

            @blk.scalar
            def _(sc):
                k = 0
                for dk in range(2):
                    for t5 in range(4):
                        pp = k % 2
                        sc.wait_ge(stn, k + 1)
                        sc.activation(qkF[:, dk * OWN + t5 * 512:dk * OWN + (t5 + 1) * 512],
                                      pqk2[:, pp * 512:(pp + 1) * 512], AF.Silu).then_inc(sa, 1)
                        k += 1
            pq.close()

        with nc.Block("pc_rope") as blk:
            @blk.vector
            def _(ve):
                for i, dst in [(0, qq), (1, lq), (2, qkk)]:
                    for c in range(2):
                        ve.tensor_scalar(hpre[:, c * OWN:(c + 1) * OWN],
                                         qkF[:, c * OWN:(c + 1) * OWN],
                                         gamC[:, i * 2 + c:i * 2 + c + 1],
                                         betC[:, i * 2 + c:i * 2 + c + 1],
                                         ALU.mult, ALU.add)
                    x1 = hpre[:, 0:OWN]; x2_ = hpre[:, OWN:2 * OWN]
                    co = cosF[:, 0:OWN]; si = sinF[:, 0:OWN]
                    o1 = dst[:, 0:OWN]; o2 = dst[:, OWN:2 * OWN]
                    ve.tensor_mul(o1, x1, co)
                    ve.tensor_mul(o2, x2_, si)
                    ve.tensor_sub(o1, o1, o2)
                    ve.tensor_mul(o2, x2_, co)
                    ve.tensor_mul(x1, x1, si)
                    ve.tensor_add(o2, o2, x1)

        with nc.Block("pc_quad") as blk:
            pq = ExitStack()
            pqd = pq.enter_context(nc.psum_tensor("pqd", [128, 2 * 512], F32)).ap()
            stn = semN("pcq_t"); sa = semN("pcq_a"); sv = semN("pcq_v")

            @blk.gpsimd
            def _(gp):
                gp.memset(P2[:, :], 0.0).then_inc(stn, 1)

            @blk.tensor
            def _(te):
                te.wait_ge(stn, 1)
                for g in range(32):
                    base = (g % 2) * 64
                    pp = g % 2
                    if g >= 2:
                        te.wait_ge(sv, g - 1)
                    for c in range(2):
                        last = te.matmul(pqd[base:base + 64, pp * 512:pp * 512 + 64],
                                         qkk[:, c * OWN + g * 64:c * OWN + (g + 1) * 64],
                                         qq[:, c * OWN + g * 64:c * OWN + (g + 1) * 64],
                                         start=(c == 0), stop=(c == 1),
                                         tile_position=(0, base))
                    last.then_inc(stn, 1)

            @blk.vector
            def _(ve):
                for g in range(32):
                    base = (g % 2) * 64
                    pp = g % 2
                    tb = g // 2
                    ve.wait_ge(stn, g + 2)
                    ve.scalar_tensor_tensor(ptmp[base:base + 64, 0:64],
                                            pqd[base:base + 64, pp * 512:pp * 512 + 64],
                                            1.0 / 64.0, qbias[base:base + 64, :],
                                            ALU.mult, ALU.add)
                    ve.tensor_relu(ptmp[base:base + 64, 64:128], ptmp[base:base + 64, 0:64])
                    ve.tensor_mul(P2[base:base + 64, tb * 128 + base:tb * 128 + base + 64],
                                  ptmp[base:base + 64, 0:64],
                                  ptmp[base:base + 64, 64:128]).then_inc(sv, 1)
            pq.close()
        dbg_sb("qkF", qkF[:, :], 128, 2 * OWN)
        dbg_sb("qq", qq[:, :], 128, 2 * OWN)
        dbg_sb("lqd", lq[:, :], 128, 2 * OWN)
        dbg_sb("qkk", qkk[:, :], 128, 2 * OWN)
        dbg_sb("P2", P2[:, :], 128, 16 * 128)
        pcH.close()
        trF_g.__exit__(None, None, None)

        pcg = ExitStack()
        Wgb = pcg.enter_context(nc.sbuf_tensor("Wgb", [128, 8 * 2048], BF16, side="right")).ap()
        gstg = pcg.enter_context(nc.sbuf_tensor("gstg", [128, 2 * 2048], BF16, side="right")).ap()
        bgP = pcg.enter_context(nc.sbuf_tensor("bgP", [1, 2048], BF16, side="right")).ap()
        pgg = pcg.enter_context(nc.psum_tensor("pgg", [128, 2 * 2048], F32)).ap()

        with nc.Block("pc_gload") as blk:
            @blk.gpsimd
            def _(gp):
                w = wload(gp, Wgb[:, :], WgT)
                gp.wait_ge(sW, w)

        with nc.Block("pc_gfold") as blk:
            s1 = semN("pcgf_t")

            @blk.tensor
            def _(te):
                for c in range(NCH):
                    for j in range(4):
                        last = te.matmul(pgg[0:1, j * 512:(j + 1) * 512], ln1bB[:, c:c + 1],
                                         Wgb[:, c * 2048 + j * 512:c * 2048 + (j + 1) * 512],
                                         start=(c == 0), stop=(c == NCH - 1))
                last.then_inc(s1, 1)

            @blk.vector
            def _(ve):
                ve.wait_ge(s1, 1)
                ve.tensor_add(bgP[:, :], pgg[0:1, 0:2048], bgR[:, :])
                for c in range(NCH):
                    ve.tensor_scalar_mul(Wgb[:, c * 2048:(c + 1) * 2048],
                                         Wgb[:, c * 2048:(c + 1) * 2048], ln1w[:, c:c + 1])

        with nc.Block("pc_gate") as blk:
            stn = semN("pcg_t"); sa = semN("pcg_a"); sSp = semN("pcg_sp")

            @blk.tensor
            def _(te):
                for b_ in range(16):
                    pp = b_ % 2
                    if b_ >= 2:
                        te.wait_ge(sa, b_ - 1)
                    for j in range(4):
                        te.matmul(pgg[:, pp * 2048 + j * 512:pp * 2048 + (j + 1) * 512],
                                  ones1[:, :], bgP[:, j * 512:(j + 1) * 512],
                                  start=True, stop=False)
                    for c in range(NCH):
                        for j in range(4):
                            last = te.matmul(pgg[:, pp * 2048 + j * 512:pp * 2048 + (j + 1) * 512],
                                             zo[:, c * OWN + b_ * 128:c * OWN + b_ * 128 + 128],
                                             Wgb[:, c * 2048 + j * 512:c * 2048 + (j + 1) * 512],
                                             start=False, stop=(c == NCH - 1))
                    last.then_inc(stn, 1)

            @blk.scalar
            def _(sc):
                for b_ in range(16):
                    pp = b_ % 2
                    sc.wait_ge(stn, b_ + 1)
                    if b_ >= 2:
                        sc.wait_ge(sSp, 16 * (b_ - 1))
                    sc.activation(gstg[:, pp * 2048:(pp + 1) * 2048],
                                  pgg[:, pp * 2048:(pp + 1) * 2048], AF.Silu).then_inc(sa, 1)

            @blk.sync
            def _(sy):
                for b_ in range(16):
                    sy.wait_ge(sa, b_ + 1)
                    sy.dma_start(out=gatehbm[b_ * 128:(b_ + 1) * 128, :],
                                 in_=gstg[:, (b_ % 2) * 2048:(b_ % 2) * 2048 + 2048]).then_inc(sSp, 16)
                sy.wait_ge(sSp, 16 * 16)
        pcg.close()
        dbg_hbm("gate", gatehbm.rearrange("(n p) k -> n p k", p=128), 16, 128, 2048)

        pcm = ExitStack()
        Wob = pcm.enter_context(nc.sbuf_tensor("Wob", [128, 16 * 1024], BF16, side="right")).ap()
        vrd2 = pcm.enter_context(nc.sbuf_tensor("vrd2", [128, 2 * 2048], BF16, side="right")).ap()
        grd2 = pcm.enter_context(nc.sbuf_tensor("grd2", [128, 2 * 2048], BF16, side="right")).ap()
        gtile = pcm.enter_context(nc.sbuf_tensor("gtile", [128, 4 * 2048], F32, side="right")).ap()
        gatF = pcm.enter_context(nc.sbuf_tensor("gatF", [128, 16 * 512], BF16, side="right")).ap()
        ogsb = pcm.enter_context(nc.sbuf_tensor("ogsb", [128, 8 * 512], BF16, side="right")).ap()
        og2 = pcm.enter_context(nc.sbuf_tensor("og2", [128, 2 * 512], BF16, side="right")).ap()
        st8 = pcm.enter_context(nc.sbuf_tensor("st8", [128, 6 * 512], F32, side="right")).ap()
        PZ = pcm.enter_context(nc.psum_tensor("PZ", [128, 4096], F32)).ap()

        def pqp_slot(tb):
            return PZ[:, (tb % 2) * 2048:(tb % 2) * 2048 + 2048]
        pogf = PZ[:, 2560:3584]
        pstA = PZ[:, 3584:4096]
        pstB = PZ[:, 2048:2560]

        TRB = [0, 512, 1024, 1536]

        def ptr_slot(kk):
            off = TRB[(kk // 4) % 4]
            return PZ[:, off + (kk % 4) * 128:off + (kk % 4) * 128 + 128]

        with nc.Block("pc_wload") as blk:
            @blk.gpsimd
            def _(gp):
                w = wload(gp, Wob[:, :], WoT)
                gp.wait_ge(sW, w)

        stn = semN("pcm_t"); sa = semN("pcm_a")
        sv = semN("pcm_v"); sv2 = semN("pcm_v2")
        sRd = semN("pcm_rd"); sSp = semN("pcm_sp")
        for sup in range(SUPS):
            Bt = sup * 36; Ba = sup * 9; Bv = sup * 4; Bv2 = sup * 18
            Br = sup * 16 * 8; Bs = sup * 16 * 8
            with nc.Block("pc_m%d" % sup) as blk:

                @blk.sync
                def _(sy):
                    for tb in range(4):
                        b_ = sup * 4 + tb
                        if tb >= 2:
                            sy.wait_ge(stn, Bt + tb - 1)
                            sy.wait_ge(sv, Bv + tb - 1)
                        sy.dma_start(out=vrd2[:, (tb % 2) * 2048:(tb % 2) * 2048 + 2048],
                                     in_=vhbm[b_ * 128:(b_ + 1) * 128, :]).then_inc(sRd, 16)
                        sy.dma_start(out=grd2[:, (tb % 2) * 2048:(tb % 2) * 2048 + 2048],
                                     in_=gatehbm[b_ * 128:(b_ + 1) * 128, :]).then_inc(sRd, 16)
                    sy.wait_ge(sv2, Bv2 + 18)
                    for do in range(8):
                        sy.dma_start(out=gauhbm[do, :, sup * 512:(sup + 1) * 512],
                                     in_=ogsb[:, do * 512:(do + 1) * 512]).then_inc(sSp, 16)
                    sy.wait_ge(sSp, Bs + 16 * 8)

                @blk.tensor
                def _(te):
                    for tb in range(4):
                        b_ = sup * 4 + tb
                        if tb >= 2:
                            te.wait_ge(sv, Bv + tb - 1)
                        te.wait_ge(sRd, Br + 16 * (2 * tb + 1))
                        pq_ = pqp_slot(tb)
                        for j in range(4):
                            for dk in range(2):
                                te.matmul(pq_[:, j * 512:(j + 1) * 512],
                                          lq[:, dk * OWN + b_ * 128:dk * OWN + b_ * 128 + 128],
                                          linkv[:, dk * 2048 + j * 512:dk * 2048 + (j + 1) * 512],
                                          start=(dk == 0), stop=False)
                            last = te.matmul(pq_[:, j * 512:(j + 1) * 512],
                                             P2[:, (b_ % 16) * 128:((b_ % 16) + 1) * 128],
                                             vrd2[:, (tb % 2) * 2048 + j * 512:(tb % 2) * 2048 + (j + 1) * 512],
                                             start=False, stop=True)
                        last.then_inc(stn, 1)
                    te.wait_ge(sv, Bv + 4)
                    for grp in range(16):
                        tb, ec0 = grp // 4, (grp % 4) * 4
                        if grp >= 4:
                            te.wait_ge(sv2, Bv2 + grp - 3)
                        for q_ in range(4):
                            kk = grp * 4 + q_
                            last = te.transpose(ptr_slot(kk),
                                                gtile[:, tb * 2048 + (ec0 + q_) * 128:
                                                      tb * 2048 + (ec0 + q_ + 1) * 128],
                                                identF[:, :])
                        last.then_inc(stn, 1)
                    te.wait_ge(sv2, Bv2 + 16)
                    for do in range(8):
                        if do >= 2:
                            te.wait_ge(sa, Ba + do - 1)
                        for ec in range(16):
                            last = te.matmul(pogf[:, (do % 2) * 512:(do % 2) * 512 + 512],
                                             Wob[:, ec * 1024 + do * 128:ec * 1024 + (do + 1) * 128],
                                             gatF[:, ec * 512:(ec + 1) * 512],
                                             start=(ec == 0), stop=(ec == 15))
                        last.then_inc(stn, 1)
                        te.wait_ge(sa, Ba + do + 1)
                        te.matmul(pstA[:, 0:512], ones128[:, :], ogsb[:, do * 512:(do + 1) * 512],
                                  start=(do == 0), stop=(do == 7))
                        last = te.matmul(pstB[:, 0:512], ones128[:, :], og2[:, (do % 2) * 512:(do % 2) * 512 + 512],
                                         start=(do == 0), stop=(do == 7))
                        last.then_inc(stn, 1)

                @blk.scalar
                def _(sc):
                    for do in range(8):
                        sc.wait_ge(stn, Bt + 4 + 16 + 2 * do + 1)
                        sc.activation(ogsb[:, do * 512:(do + 1) * 512],
                                      pogf[:, (do % 2) * 512:(do % 2) * 512 + 512],
                                      AF.Identity, bias=boC[:, do:do + 1])
                        if do >= 2:
                            sc.wait_ge(stn, Bt + 4 + 16 + 2 * (do - 1))
                        sc.activation(og2[:, (do % 2) * 512:(do % 2) * 512 + 512],
                                      ogsb[:, do * 512:(do + 1) * 512], AF.Square).then_inc(sa, 1)
                    sc.wait_ge(sv2, Bv2 + 17)
                    sc.activation(st8[:, 2 * 512:3 * 512], st8[:, 1 * 512:2 * 512], AF.Sqrt).then_inc(sa, 1)

                @blk.vector
                def _(ve):
                    for tb in range(4):
                        ve.wait_ge(stn, Bt + tb + 1)
                        ve.wait_ge(sRd, Br + 16 * (2 * tb + 2))
                        ve.tensor_tensor(gtile[:, tb * 2048:(tb + 1) * 2048], pqp_slot(tb),
                                         grd2[:, (tb % 2) * 2048:(tb % 2) * 2048 + 2048],
                                         ALU.mult)
                        ve.tensor_copy(st8[:, 5 * 512:5 * 512 + 128],
                                       gtile[:, tb * 2048:tb * 2048 + 128]).then_inc(sv, 1)
                    for grp in range(16):
                        tb, ec0 = grp // 4, (grp % 4) * 4
                        base = TRB[grp % 4]
                        ve.wait_ge(stn, Bt + 4 + grp + 1)
                        ve.tensor_copy(
                            gatF.rearrange("p (e r) -> p e r", e=16)[
                                :, ec0:ec0 + 4, tb * 128:(tb + 1) * 128],
                            PZ[:, base:base + 512].rearrange("p (e r) -> p e r", e=4)
                        ).then_inc(sv2, 1)
                    ve.wait_ge(stn, Bt + 4 + 16 + 16)
                    sl = [st8[:, j * 512:(j + 1) * 512] for j in range(6)]
                    ve.tensor_single_scalar(sl[0], pstA[:, 0:512], 1.0 / D, ALU.mult)
                    ve.tensor_single_scalar(sl[1], pstB[:, 0:512], 1.0 / D, ALU.mult)
                    ve.tensor_tensor(sl[3], sl[0], sl[0], ALU.mult)
                    ve.tensor_sub(sl[1], sl[1], sl[3])
                    ve.tensor_single_scalar(sl[1], sl[1], EPS, ALU.add)
                    ve.engine_nop()
                    ve.engine_nop().then_inc(sv2, 1)
                    ve.wait_ge(sa, Ba + 8 + 1)
                    ve.reciprocal(sl[3], sl[2])
                    ve.tensor_tensor(sl[5], sl[3], sl[3], ALU.mult)
                    ve.tensor_tensor(sl[5], sl[5], sl[1], ALU.mult)
                    ve.tensor_scalar(sl[5], sl[5], -0.5, 1.5, ALU.mult, ALU.add)
                    ve.tensor_tensor(sl[3], sl[3], sl[5], ALU.mult)
                    for do in range(8):
                        o_ = ogsb[:, do * 512:(do + 1) * 512]
                        ve.tensor_sub(sl[4], o_, sl[0])
                        ve.tensor_tensor(sl[4], sl[4], sl[3], ALU.mult)
                        ve.tensor_scalar(o_, sl[4], ln2w[:, do:do + 1], ln2b[:, do:do + 1],
                                         ALU.mult, ALU.add)
                    ve.engine_nop().then_inc(sv2, 1)
        for _tb in range(4):
            dbg_sb("gtile%d" % _tb, gtile[:, _tb * 2048:(_tb + 1) * 2048], 128, 2048)
        for _ec in (0, 7, 15):
            dbg_sb("gatF%d" % _ec, gatF[:, _ec * 512:(_ec + 1) * 512], 128, 512)
        pcm.close()
        pc.close()
        dbg_hbm("gau", gauhbm, 8, 128, OWN)

    if PHASES >= 7:
        pd = ExitStack()
        Owb = pd.enter_context(nc.sbuf_tensor("Owb", [128, 8 * 1024], BF16, side="right")).ap()
        ard = pd.enter_context(nc.sbuf_tensor("ard", [128, 8 * 512], BF16, side="right")).ap()
        grd = pd.enter_context(nc.sbuf_tensor("grd", [128, 8 * 512], BF16, side="right")).ap()
        ogf = pd.enter_context(nc.sbuf_tensor("ogf", [128, 8 * 512], BF16, side="right")).ap()
        xrd = pd.enter_context(nc.sbuf_tensor("xrd", [128, 2 * 512], F32, side="right")).ap()
        fout = pd.enter_context(nc.sbuf_tensor("fout", [128, 2 * 512], F32, side="right")).ap()
        ftmp = pd.enter_context(nc.sbuf_tensor("ftmp", [128, 512], F32, side="right")).ap()
        pof = pd.enter_context(nc.psum_tensor("pof", [128, 2 * 512], F32)).ap()

        with nc.Block("pd_load") as blk:
            @blk.gpsimd
            def _(gp):
                w = wload(gp, Owb[:, :], owT)
                gp.wait_ge(sW, w)

        stn = semN("pd_t"); sv = semN("pd_v"); sa = semN("pd_a")
        sRd = semN("pd_rd"); sO = semN("pd_o")
        Bt = Ba = Bv = Br = Bo = 0
        for e5 in range(4):
            mch = MCH512[e5]
            with nc.Block("pd_%d" % e5) as blk:

                @blk.sync
                def _(sy):
                    for mc in range(mch):
                        sy.dma_start(out=ard[:, mc * 512:(mc + 1) * 512],
                                     in_=attnhbm[mc, :, e5 * 512:(e5 + 1) * 512]).then_inc(sRd, 16)
                    for do in range(8):
                        sy.dma_start(out=grd[:, do * 512:(do + 1) * 512],
                                     in_=gauhbm[do, :, e5 * 512:(e5 + 1) * 512]).then_inc(sRd, 16)
                    for dc in range(8):
                        if dc >= 2:
                            sy.wait_ge(sv, Bv + dc - 1)
                        sy.dma_start(out=xrd[:, (dc % 2) * 512:(dc % 2) * 512 + 512],
                                     in_=xT[dc * 128:(dc + 1) * 128, e5 * 512:(e5 + 1) * 512]
                                     ).then_inc(sRd, 16)
                        sy.wait_ge(sv, Bv + dc + 1)
                        sy.dma_start(out=outT[dc * 128:(dc + 1) * 128, e5 * 512:(e5 + 1) * 512],
                                     in_=fout[:, (dc % 2) * 512:(dc % 2) * 512 + 512]).then_inc(sO, 16)
                    sy.wait_ge(sO, Bo + 16 * 8)

                @blk.tensor
                def _(te):
                    te.wait_ge(sRd, Br + 16 * mch)
                    for oc in range(mch):
                        if oc >= 2:
                            te.wait_ge(sa, Ba + oc - 1)
                        for mc in range(mch):
                            last = te.matmul(pof[:, (oc % 2) * 512:(oc % 2) * 512 + 512],
                                             Owb[:, mc * 1024 + oc * 128:mc * 1024 + (oc + 1) * 128],
                                             ard[:, mc * 512:(mc + 1) * 512],
                                             start=(mc == 0), stop=(mc == mch - 1))
                        last.then_inc(stn, 1)

                @blk.scalar
                def _(sc):
                    for oc in range(mch):
                        sc.wait_ge(stn, Bt + oc + 1)
                        sc.activation(ogf[:, oc * 512:(oc + 1) * 512],
                                      pof[:, (oc % 2) * 512:(oc % 2) * 512 + 512],
                                      AF.Identity, bias=obC[:, oc:oc + 1]).then_inc(sa, 1)

                @blk.vector
                def _(ve):
                    for dc in range(8):
                        ve.wait_ge(sRd, Br + 16 * (mch + 8 + dc + 1))
                        if dc < mch:
                            ve.wait_ge(sa, Ba + dc + 1)
                        if dc >= 2:
                            ve.wait_ge(sO, Bo + 16 * (dc - 1))
                        xs = xrd[:, (dc % 2) * 512:(dc % 2) * 512 + 512]
                        fo = fout[:, (dc % 2) * 512:(dc % 2) * 512 + 512]
                        mu_ = MU[:, e5 * 512:(e5 + 1) * 512]
                        rs_ = RSTD[:, e5 * 512:(e5 + 1) * 512]
                        ve.tensor_sub(ftmp[:, :], xs, mu_)
                        ve.tensor_tensor(ftmp[:, :], ftmp[:, :], rs_, ALU.mult)
                        ve.tensor_add(fo, ftmp[:, :], xs)
                        ve.tensor_add(fo, fo, grd[:, dc * 512:(dc + 1) * 512])
                        if dc < mch:
                            ve.tensor_add(fo, fo, ogf[:, dc * 512:(dc + 1) * 512])
                        ve.tensor_copy(ftmp[:, 0:128], fo[:, 0:128]).then_inc(sv, 1)
            Bt += mch; Ba += mch; Bv += 8; Br += 16 * (mch + 16); Bo += 16 * 8
        pd.close()

    st.close()
    return nc, dbg


_NC_CACHE = {}


def _get_nc():
    key = (PHASES, tuple(DEBUG))
    if key not in _NC_CACHE:
        _NC_CACHE[key] = build_nc()
    return _NC_CACHE[key]


def _in_map_for(inputs, cid):
    b, h = cid // 2, cid % 2
    P = _perm_for(h)
    f32 = np.float32
    x = np.asarray(inputs["L0_attn_x"])[b][P]
    gamma = np.asarray(inputs["L0_attn_gamma"])
    beta = np.asarray(inputs["L0_attn_beta"])

    def colchunks(vec):
        return np.ascontiguousarray(np.asarray(vec, f32).reshape(8, 128).T)

    return {
        "xT": np.ascontiguousarray(x.T, dtype=f32),
        "posrow": np.ascontiguousarray(P[None, :OWN].astype(f32)),
        "poscol": np.ascontiguousarray(P.astype(f32).reshape(32, 128).T),
        "WqkT": np.ascontiguousarray(np.asarray(inputs["L0_attn_Wqk"], f32).T),
        "WvT": np.ascontiguousarray(np.asarray(inputs["L0_attn_Wv"], f32).T),
        "WgT": np.ascontiguousarray(np.asarray(inputs["L0_attn_Wg"], f32).T),
        "WoT": np.ascontiguousarray(np.asarray(inputs["L0_attn_Wo"], f32).T),
        "qwT": np.ascontiguousarray(np.asarray(inputs["L0_ffn_q_w"], f32).T),
        "kwT": np.ascontiguousarray(np.asarray(inputs["L0_ffn_k_w"], f32).T),
        "vwT": np.ascontiguousarray(np.asarray(inputs["L0_ffn_v_w"], f32).T),
        "owT": np.ascontiguousarray(np.asarray(inputs["L0_ffn_o_w"], f32).T),
        "pos_emb": np.ascontiguousarray(np.asarray(inputs["L0_attn_pos_emb"], f32)[None, :]),
        "gammaC": np.ascontiguousarray(gamma.reshape(4, 2, 128).transpose(2, 0, 1).reshape(128, 8).astype(f32)),
        "betaC": np.ascontiguousarray(beta.reshape(4, 2, 128).transpose(2, 0, 1).reshape(128, 8).astype(f32)),
        "g3_row": np.ascontiguousarray(gamma[3][None, :].astype(f32)),
        "b3_row": np.ascontiguousarray(beta[3][None, :].astype(f32)),
        "bqk_row": np.ascontiguousarray(np.asarray(inputs["L0_attn_bqk"], f32)[None, :]),
        "bqk_col": np.ascontiguousarray(np.asarray(inputs["L0_attn_bqk"], f32).reshape(2, 128).T),
        "bv_row": np.ascontiguousarray(np.asarray(inputs["L0_attn_bv"], f32)[None, :]),
        "bg_row": np.ascontiguousarray(np.asarray(inputs["L0_attn_bg"], f32)[None, :]),
        "vb_row": np.ascontiguousarray(np.asarray(inputs["L0_ffn_v_b"], f32)[None, :]),
        "ln1w_c": colchunks(inputs["L0_attn_ln1_w"]),
        "ln1b_c": colchunks(inputs["L0_attn_ln1_b"]),
        "lnfw_c": colchunks(inputs["L0_ffn_norm_w"]),
        "lnfb_c": colchunks(inputs["L0_ffn_norm_b"]),
        "ln2w_c": colchunks(inputs["L0_attn_ln2_w"]),
        "ln2b_c": colchunks(inputs["L0_attn_ln2_b"]),
        "qb_c": colchunks(inputs["L0_ffn_q_b"]),
        "kb_c": colchunks(inputs["L0_ffn_k_b"]),
        "bo_c": colchunks(inputs["L0_attn_bo"]),
        "ob_c": colchunks(inputs["L0_ffn_o_b"]),
    }


def kernel(**inputs):
    nc, dbg = _get_nc()
    in_maps = [_in_map_for(inputs, cid) for cid in range(8)]
    res = run_bass_kernel_spmd(nc, in_maps, core_ids=list(range(8)))
    out = np.zeros((4, 4096, 1024), dtype=np.asarray(res.results[0]["outT"]).dtype)
    for cid in range(8):
        b, h = cid // 2, cid % 2
        P = _perm_for(h)
        out[b][P[:OWN]] = res.results[cid]["outT"].T
    kernel._last_results = res.results
    return out
